# version 24 (final)
# baseline (speedup 1.0000x reference)
import numpy as np
from contextlib import ExitStack

import concourse.bass as bass
import concourse.mybir as mybir
from concourse.bass_utils import run_bass_kernel_spmd

F32 = mybir.dt.float32
BF16 = mybir.dt.bfloat16
AF = mybir.ActivationFunctionType
ALU = mybir.AluOpType
AX = mybir.AxisListType

S = 4096
OWN = 2048
DM = 1024
NH = 16
DH = 64
FF = 4096
NDC = DM // 128
NT = S // 128
NTO = OWN // 128
QC = 512
NQC = OWN // QC
NG = 4
KC = S // 128
VW = 65
INV_DM = 1.0 / DM


class Sems:
    def __init__(self, nc, st):
        self.nc = nc
        self.st = st
        self.m = {}

    def new(self, *names):
        for name in names:
            h = self.st.enter_context(self.nc.semaphore(name))
            self.m[name] = [h, 0]

    def inc(self, name, instr, k=1):
        e = self.m[name]
        instr.then_inc(e[0], k)
        e[1] += k
        return instr

    def h(self, name):
        return self.m[name][0]

    def n(self, name):
        return self.m[name][1]


def _finish(nc, sb, waits):
    with nc.Block() as blk:
        @blk.sync
        def _(sync):
            for nm, v in waits:
                sync.wait_ge(sb.h(nm), v)


def build_nc(phases=5, att_g=NQC * NG):
    nc = bass.Bass("TRN2", target_bir_lowering=False)
    n_qc_run = (att_g + NG - 1) // NG

    x_d = nc.dram_tensor("x", [S, DM], F32, kind="ExternalInput").ap()
    wqkv_d = nc.dram_tensor("wqkv", [DM, 3 * DM], F32, kind="ExternalInput").ap()
    wo_d = nc.dram_tensor("wo", [DM, DM], F32, kind="ExternalInput").ap()
    w1_d = nc.dram_tensor("w1", [DM, FF], F32, kind="ExternalInput").ap()
    w2_d = nc.dram_tensor("w2", [FF, DM], F32, kind="ExternalInput").ap()
    bqkvpt_d = nc.dram_tensor("bqkv_pt", [128, 16], F32, kind="ExternalInput").ap()
    bvb_d = nc.dram_tensor("bvb_t", [128, DM], F32, kind="ExternalInput").ap()
    bo_d = nc.dram_tensor("bo_t", [128, DM], F32, kind="ExternalInput").ap()
    b1pt_d = nc.dram_tensor("b1_pt", [128, 32], F32, kind="ExternalInput").ap()
    b2_d = nc.dram_tensor("b2_t", [128, DM], F32, kind="ExternalInput").ap()
    gam_d = nc.dram_tensor("gam_t", [128, DM], F32, kind="ExternalInput").ap()
    bet_d = nc.dram_tensor("bet_t", [128, DM], F32, kind="ExternalInput").ap()
    csq_d = nc.dram_tensor("csq", [128, OWN], F32, kind="ExternalInput").ap()
    snq_d = nc.dram_tensor("snq", [128, OWN], F32, kind="ExternalInput").ap()
    csk_d = nc.dram_tensor("csk", [128, S], F32, kind="ExternalInput").ap()
    snk_d = nc.dram_tensor("snk", [128, S], F32, kind="ExternalInput").ap()
    ident_d = nc.dram_tensor("ident", [128, 128], F32, kind="ExternalInput").ap()
    out_d = nc.dram_tensor("out", [OWN, DM], F32, kind="ExternalOutput").ap()

    ln1_s = nc.dram_tensor("ln1_s", [128, NDC * S], BF16, kind="Internal").ap()
    q_s = nc.dram_tensor("q_s", [128, 8 * OWN], BF16, kind="Internal").ap()
    x1_s = nc.dram_tensor("x1_s", [128, NTO * DM], F32, kind="Internal").ap()
    ln3_s = nc.dram_tensor("ln3_s", [128, NDC * OWN], BF16, kind="Internal").ap()
    wob_s = nc.dram_tensor("wob_s", [128, 8 * 1024], BF16, kind="Internal").ap()
    w2b_s = nc.dram_tensor("w2b_s", [128, 32 * 1024], BF16, kind="Internal").ap()
    w1b_s = nc.dram_tensor("w1b_s", [128, 8 * 4096], BF16, kind="Internal").ap()

    ln1s_r = ln1_s.rearrange("p (dc t) -> p dc t", t=S)
    qs_r = q_s.rearrange("p (ch t) -> p ch t", t=OWN)
    ln3s_r = ln3_s.rearrange("p (dc t) -> p dc t", t=OWN)
    w1bs_r = w1b_s.rearrange("p (h d u) -> p h d u", d=8, u=512)

    st = ExitStack()
    sb = Sems(nc, st)
    sb.new("P0d",
           "xd0", "xd1", "aS", "dR", "pM", "dV", "aL", "aE", "dN", "pT", "aV",
           "lo0", "lo1",
           "w2d0", "w2d1", "w2c", "l2d0", "l2d1", "c2d0", "c2d1", "p2m", "d2m", "p2c", "a2q", "p2n", "r2e", "r2s",
           "d2x", "q2o", "q2o1", "nbx", "negB", "v2d0", "v2d1", "p2v", "d2v",
           "q3d", "s3", "s3a", "e3a", "e3b", "w3", "ev3", "b3", "n3", "dnd",
           "wcd", "wcc", "wco",
           "x4d0", "x4d1", "w4d", "p4o", "d4a", "d4h", "a4q2", "d4r2",
           "p4s2", "d4v2", "a4e2", "d4t", "d4u", "p4b", "p4x", "sx10", "sx11",
           "a4q3", "d4r3", "p4s3", "d4v3", "a4e3", "d4n", "p4T", "a4V",
           "s3o0", "s3o1",
           "l5d", "l5d1", "w51d", "w52d", "p5f1", "a5g", "p5f2",
           "x5d0", "x5d1", "d5e", "p5o", "o50", "o51")

    L = dict(side="left")
    eps_t = st.enter_context(nc.sbuf_tensor("eps_t", [128, 1], F32, **L))
    negB_t = st.enter_context(nc.sbuf_tensor("negB_t", [128, 1], F32, **L))
    ones_f = st.enter_context(nc.sbuf_tensor("ones_f", [1, 128], F32, **L))
    ones_b = st.enter_context(nc.sbuf_tensor("ones_b", [1, 128], BF16, **L))
    ones4 = st.enter_context(nc.sbuf_tensor("ones4", [128, 128], BF16, **L))
    ident_b = st.enter_context(nc.sbuf_tensor("ident_b", [128, 128], BF16, **L))
    bqkv_pt = st.enter_context(nc.sbuf_tensor("bqkv_pt_s", [128, 16], F32, **L))
    b1_pt = st.enter_context(nc.sbuf_tensor("b1_pt_s", [128, 32], F32, **L))
    s1 = st.enter_context(nc.sbuf_tensor("s1", [128, NT], F32, **L))
    s2 = st.enter_context(nc.sbuf_tensor("s2", [128, NT], F32, **L))
    t1c = st.enter_context(nc.sbuf_tensor("t1c", [128, NT], F32, **L))
    mean1 = st.enter_context(nc.sbuf_tensor("mean1", [128, NT], F32, **L))
    var1 = st.enter_context(nc.sbuf_tensor("var1", [128, NT], F32, **L))
    lnv1 = st.enter_context(nc.sbuf_tensor("lnv1", [128, NT], F32, **L))
    rstd1 = st.enter_context(nc.sbuf_tensor("rstd1", [128, NT], F32, **L))
    s1b = st.enter_context(nc.sbuf_tensor("s1b", [128, NTO], F32, **L))
    s2b = st.enter_context(nc.sbuf_tensor("s2b", [128, NTO], F32, **L))
    t1b = st.enter_context(nc.sbuf_tensor("t1b", [128, NTO], F32, **L))
    mn2 = st.enter_context(nc.sbuf_tensor("mn2", [128, NTO], F32, **L))
    vr2 = st.enter_context(nc.sbuf_tensor("vr2", [128, NTO], F32, **L))
    lv2 = st.enter_context(nc.sbuf_tensor("lv2", [128, NTO], F32, **L))
    rs2 = st.enter_context(nc.sbuf_tensor("rs2", [128, NTO], F32, **L))
    s1d = st.enter_context(nc.sbuf_tensor("s1d", [128, NTO], F32, **L))
    s2d = st.enter_context(nc.sbuf_tensor("s2d", [128, NTO], F32, **L))
    t1d = st.enter_context(nc.sbuf_tensor("t1d", [128, NTO], F32, **L))
    mn3 = st.enter_context(nc.sbuf_tensor("mn3", [128, NTO], F32, **L))
    vr3 = st.enter_context(nc.sbuf_tensor("vr3", [128, NTO], F32, **L))
    lv3 = st.enter_context(nc.sbuf_tensor("lv3", [128, NTO], F32, **L))
    rs3 = st.enter_context(nc.sbuf_tensor("rs3", [128, NTO], F32, **L))
    normc = st.enter_context(nc.sbuf_tensor("normc", [128, 48], F32, **L))
    nbt = st.enter_context(nc.sbuf_tensor("nbt", [128, 4], F32, **L))
    nbg = st.enter_context(nc.sbuf_tensor("nbg", [128, 1], BF16, **L))
    nbb = st.enter_context(nc.sbuf_tensor("nbb", [1, 4], BF16, **L))
    st_kv = ExitStack()
    k_fm = st_kv.enter_context(nc.sbuf_tensor("k_fm", [128, 8 * S], BF16, **L))
    v_tm = st_kv.enter_context(nc.sbuf_tensor("v_tm", [128, KC * 16 * VW], BF16, **L))
    q_qc = st_kv.enter_context(nc.sbuf_tensor("q_qc", [128, 4096], BF16, **L))
    st_w2 = ExitStack()
    wqk_bf = st_w2.enter_context(
        nc.sbuf_tensor("wqk_bf", [128, 8192], BF16, side="right"))
    wstg = st_w2.enter_context(
        nc.sbuf_tensor("wstg", [128, 2048], F32, side="right"))

    st_bvb = ExitStack()
    bvb_t = st_bvb.enter_context(
        nc.sbuf_tensor("bvb_s", [128, DM], F32, side="right"))
    st_id = ExitStack()
    identf = st_id.enter_context(
        nc.sbuf_tensor("identf", [128, 128], F32, side="right"))

    with nc.Block() as blk:
        @blk.sync
        def _(sync):
            for src, dst in [(bqkvpt_d, bqkv_pt), (b1pt_d, b1_pt),
                             (bvb_d, bvb_t), (ident_d, identf)]:
                sb.inc("P0d", sync.dma_start(dst[:, :], src), 16)

        @blk.vector
        def _(vector):
            vector.wait_ge(sb.h("P0d"), sb.n("P0d"))
            vector.tensor_copy(ident_b[:, :], identf[:, :])
            vector.memset(eps_t[:, :], 1e-5)
            vector.memset(ones_f[:, :], 1.0)
            vector.memset(ones_b[:, :], 1.0)
            vector.memset(ones4[:, :], 0.0)
            vones = v_tm[:, :].rearrange("p (kc h u) -> p kc h u", h=16, u=VW)
            vector.memset(vones[:, :, :, 64:65], 1.0)

    with nc.Block() as blk:
        @blk.vector
        def _(vector):
            for j in range(4):
                vector.memset(ones4[32 * j:32 * (j + 1), 32 * j:32 * j + 1], 1.0)

    st_id.close()

    with (
        nc.sbuf_tensor("xt1", [128, 2 * DM], F32, side="right") as xt1,
        nc.sbuf_tensor("lnbf1", [128, 2 * DM], BF16, side="right") as lnbf1,
        nc.sbuf_tensor("stg1", [128, 2 * DM], BF16, side="right") as stg1,
        nc.psum_tensor("pst1a", [128, 1024], BF16) as pst1a,
        nc.psum_tensor("pst1b", [128, 1024], BF16) as pst1b,
        nc.psum_tensor("jnk1", [128, 1024], F32) as junk,
        nc.Block() as blk,
    ):
        pst1 = [pst1a, pst1b]
        xb = lambda t: xt1[:, (t % 2) * DM:(t % 2) * DM + DM]
        lb = lambda t: lnbf1[:, (t % 2) * DM:(t % 2) * DM + DM]
        gb = lambda t: stg1[:, (t % 2) * DM:(t % 2) * DM + DM]

        @blk.sync
        def _(sync):
            for t in range(NT):
                if t >= 2:
                    sync.wait_ge(sb.h("dN"), t - 1)
                sb.inc("xd%d" % (t % 2), sync.dma_start(
                    xb(t), x_d[t * 128:(t + 1) * 128, :]), 16)

        @blk.scalar
        def _(scalar):
            def evac(tp):
                scalar.wait_ge(sb.h("pT"), tp + 1)
                if tp >= 2:
                    scalar.wait_ge(sb.h("lo%d" % (tp % 2)), 16 * (tp // 2))
                sb.inc("aV", scalar.copy(gb(tp), pst1[tp % 2][:, :]))

            for t in range(NT):
                if t >= 2:
                    evac(t - 2)
                scalar.wait_ge(sb.h("xd%d" % (t % 2)), 16 * (t // 2 + 1))
                sb.inc("aS", scalar.activation(
                    junk[:, :], xb(t), AF.Square, accum_out=s2[:, t:t + 1]))
                scalar.wait_ge(sb.h("dR"), t + 1)
                scalar.activation(mean1[:, t:t + 1], s1[:, t:t + 1], AF.Copy,
                                  scale=INV_DM)
                sb.inc("pM", scalar.activation(
                    t1c[:, t:t + 1], s1[:, t:t + 1], AF.Square, scale=0.03125))
                scalar.wait_ge(sb.h("dV"), t + 1)
                sb.inc("aL", scalar.activation(
                    lnv1[:, t:t + 1], var1[:, t:t + 1], AF.Ln, bias=eps_t[:, :]))
                scalar.wait_ge(sb.h("aL"), t + 1)
                sb.inc("aE", scalar.activation(
                    rstd1[:, t:t + 1], lnv1[:, t:t + 1], AF.Exp, scale=-0.5))
            evac(NT - 2)
            evac(NT - 1)

        @blk.vector
        def _(vector):
            for t in range(NT):
                if 2 <= t < 10:
                    idx = t - 2
                    vector.wait_ge(sb.h("w2d%d" % (idx % 2)),
                                   16 * (idx // 2 + 1))
                    sb.inc("w2c", vector.tensor_copy(
                        wqk_bf[:, (idx % 8) * 1024:(idx % 8) * 1024 + 1024],
                        wstg[:, (idx % 2) * 1024:(idx % 2) * 1024 + 1024]))
                vector.wait_ge(sb.h("xd%d" % (t % 2)), 16 * (t // 2 + 1))
                sb.inc("dR", vector.reduce_sum(s1[:, t:t + 1], xb(t), axis=AX.X))
                vector.wait_ge(sb.h("pM"), t + 1)
                vector.wait_ge(sb.h("aS"), t + 1)
                sb.inc("dV", vector.tensor_scalar(
                    var1[:, t:t + 1], s2[:, t:t + 1], t1c[:, t:t + 1], INV_DM,
                    ALU.subtract, ALU.mult))
                vector.wait_ge(sb.h("aE"), t + 1)
                if t >= 2:
                    vector.wait_ge(sb.h("pT"), t - 1)
                sb.inc("dN", vector.tensor_scalar(
                    lb(t), xb(t), mean1[:, t:t + 1], rstd1[:, t:t + 1],
                    ALU.subtract, ALU.mult))

        @blk.gpsimd
        def _(gpsimd):
            for t in range(NT):
                if t < 8:
                    idx = t
                    if idx >= 2:
                        gpsimd.wait_ge(sb.h("w2c"), idx - 1)
                    sl = (idx % 2) * 1024
                    sb.inc("w2d%d" % (idx % 2), gpsimd.dma_start(
                        wstg[:, sl:sl + 1024],
                        wqkv_d[idx * 128:(idx + 1) * 128, 0:1024]), 16)
                if t >= 1:
                    ts_ = t - 1
                    gpsimd.wait_ge(sb.h("aV"), ts_ + 1)
                    sb.inc("lo%d" % (ts_ % 2), gpsimd.dma_start(
                        ln1s_r[:, :, ts_ * 128:(ts_ + 1) * 128],
                        gb(ts_).rearrange("p (dc u) -> p dc u", u=128)), 16)
            ts_ = NT - 1
            gpsimd.wait_ge(sb.h("aV"), ts_ + 1)
            sb.inc("lo%d" % (ts_ % 2), gpsimd.dma_start(
                ln1s_r[:, :, ts_ * 128:(ts_ + 1) * 128],
                gb(ts_).rearrange("p (dc u) -> p dc u", u=128)), 16)

        @blk.tensor
        def _(tensor):
            for t in range(NT):
                tensor.wait_ge(sb.h("dN"), t + 1)
                if t >= 2:
                    tensor.wait_ge(sb.h("aV"), t - 1)
                last = None
                for dc in range(NDC):
                    last = tensor.transpose(
                        pst1[t % 2][:, dc * 128:(dc + 1) * 128],
                        lb(t)[:, dc * 128:(dc + 1) * 128], ident_b[:, :])
                sb.inc("pT", last)

    if phases <= 1:
        _finish(nc, sb, [("lo0", 16 * 16), ("lo1", 16 * 16)])
        return nc

    QK_ITERS = [(0, ts, c) for ts in range(4) for c in range(4)] + \
               [(1, ts, c) for ts in range(8) for c in range(4)]

    def giter(sec, ts, c):
        return (ts * 4 + c) if sec == 0 else (16 + ts * 4 + c)

    def gts(sec, ts):
        return ts if sec == 0 else 4 + ts

    with (
        nc.sbuf_tensor("lnbuf2", [128, 2 * 4096], BF16, side="right") as lnbuf2,
        nc.sbuf_tensor("scrm", [128, 2 * 512], F32, side="right") as scrm,
        nc.sbuf_tensor("qout", [128, 2 * 1024], BF16, side="right") as qout,
        nc.sbuf_tensor("sqeo", [128, 1024], BF16, side="right") as sqeo,
        nc.sbuf_tensor("csb", [128, 2 * 512], F32, side="right") as csb,
        nc.sbuf_tensor("snb", [128, 2 * 512], F32, side="right") as snb,
        nc.sbuf_tensor("lnvb", [128, 2 * DM], BF16, side="right") as lnvb,
        nc.psum_tensor("psE0", [128, 512], F32) as psE0,
        nc.psum_tensor("psO0", [128, 512], F32) as psO0,
        nc.psum_tensor("psE1", [128, 512], F32) as psE1,
        nc.psum_tensor("psO1", [128, 512], F32) as psO1,
        nc.psum_tensor("psN", [128, 512], F32) as psN,
        nc.psum_tensor("psTb", [128, 128], BF16) as psTb,
        nc.Block() as blk,
    ):
        psEs = [psE0, psE1]
        psOs = [psO0, psO1]
        psV = [[psE0, psO0], [psE1, psO1]]

        @blk.gpsimd
        def _(gpsimd):
            for idx in range(8, 24):
                sec = 1 if idx < 16 else 2
                dc = idx % 8
                gpsimd.wait_ge(sb.h("w2c"), idx - 1)
                sl = (idx % 2) * 1024
                sb.inc("w2d%d" % (idx % 2), gpsimd.dma_start(
                    wstg[:, sl:sl + 1024],
                    wqkv_d[dc * 128:(dc + 1) * 128,
                           sec * 1024:(sec + 1) * 1024]), 16)
            gpsimd.wait_ge(sb.h("nbx"), 1)
            sb.inc("nbx", gpsimd.tensor_copy(nbg[:, :], nbt[:, 0:1]))

        @blk.sync
        def _(sync):
            sync.wait_ge(sb.h("lo0"), 16 * (NT // 2))
            sync.wait_ge(sb.h("lo1"), 16 * (NT // 2))
            def emit_qout(tsq):
                for cq in range(4):
                    itq = tsq * 4 + cq
                    sync.wait_ge(sb.h("p2c"), itq + 1)
                    sl2 = (itq % 2) * 1024
                    nm = "q2o" if itq % 2 == 0 else "q2o1"
                    sb.inc(nm, sync.dma_start(
                        qs_r[:, cq, tsq * 512:(tsq + 1) * 512],
                        qout[:, sl2:sl2 + 512]), 16)
                    sb.inc(nm, sync.dma_start(
                        qs_r[:, 4 + cq, tsq * 512:(tsq + 1) * 512],
                        qout[:, sl2 + 512:sl2 + 1024]), 16)

            for sec in range(2):
                nts = 4 if sec == 0 else 8
                for ts in range(nts):
                    gt = gts(sec, ts)
                    if sec == 0 and ts >= 1:
                        emit_qout(ts - 1)
                    if gt >= 2:
                        sync.wait_ge(sb.h("p2m"), 4 * (gt - 1))
                    sl = (gt % 2)
                    sb.inc("l2d%d" % sl, sync.dma_start(
                        lnbuf2[:, sl * 4096:sl * 4096 + 4096].rearrange(
                            "p (dc u) -> p dc u", u=512),
                        ln1s_r[:, :, ts * 512:ts * 512 + 512]), 16)
                    if gt >= 2:
                        sync.wait_ge(sb.h("d2m"), 4 * (gt - 1))
                    cs_src = csq_d if sec == 0 else csk_d
                    sn_src = snq_d if sec == 0 else snk_d
                    sb.inc("c2d%d" % sl, sync.dma_start(
                        csb[:, sl * 512:sl * 512 + 512],
                        cs_src[:, ts * 512:(ts + 1) * 512]), 16)
                    sb.inc("c2d%d" % sl, sync.dma_start(
                        snb[:, sl * 512:sl * 512 + 512],
                        sn_src[:, ts * 512:(ts + 1) * 512]), 16)
                if sec == 0:
                    emit_qout(3)
            for tc in range(NT):
                if tc >= 2:
                    sync.wait_ge(sb.h("p2v"), tc - 1)
                sb.inc("v2d%d" % (tc % 2), sync.dma_start(
                    lnvb[:, (tc % 2) * DM:(tc % 2) * DM + DM].rearrange(
                        "p (dc u) -> p dc u", u=128),
                    ln1s_r[:, :, tc * 128:(tc + 1) * 128]), 16)
            sync.wait_ge(sb.h("q2o"), 16 * 16)
            sync.wait_ge(sb.h("q2o1"), 16 * 16)
            sb.inc("q3d", sync.dma_start(
                q_qc.rearrange("p (ch u) -> p ch u", u=512),
                qs_r[:, :, 0:512]), 16)

        @blk.tensor
        def _(tensor):
            for (sec, ts, c) in QK_ITERS:
                g2 = giter(sec, ts, c)
                gt = gts(sec, ts)
                tensor.wait_ge(sb.h("w2c"), 8 * (sec + 1))
                sl = gt % 2
                tensor.wait_ge(sb.h("l2d%d" % sl), 16 * (gt // 2 + 1))
                if g2 >= 2:
                    tensor.wait_ge(sb.h("d2m"), g2 - 1)
                psE = psEs[g2 % 2]
                psO = psOs[g2 % 2]
                for dc in range(NDC):
                    tensor.matmul(
                        psE[:, :],
                        wqk_bf[:, dc * 1024 + c * 128:dc * 1024 + c * 128 + 128],
                        lnbuf2[:, sl * 4096 + dc * 512:sl * 4096 + (dc + 1) * 512],
                        start=(dc == 0), stop=(dc == NDC - 1))
                last = None
                for dc in range(NDC):
                    o = dc * 1024 + 512 + c * 128
                    last = tensor.matmul(
                        psO[:, :], wqk_bf[:, o:o + 128],
                        lnbuf2[:, sl * 4096 + dc * 512:sl * 4096 + (dc + 1) * 512],
                        start=(dc == 0), stop=(dc == NDC - 1))
                sb.inc("p2m", last)
                if g2 >= 2:
                    pidx = g2 - 2
                    tensor.wait_ge(sb.h("a2q"), pidx + 1)
                    if pidx >= 1:
                        tensor.wait_ge(sb.h("d2x"), pidx)
                    tensor.matmul(psN[:, :], ones4[:, :], sqeo[:, 0:512],
                                  start=True, stop=False)
                    sb.inc("p2n", tensor.matmul(
                        psN[:, :], ones4[:, :], sqeo[:, 512:1024],
                        start=False, stop=True))
            for pidx in (46, 47):
                tensor.wait_ge(sb.h("a2q"), pidx + 1)
                tensor.wait_ge(sb.h("d2x"), pidx)
                tensor.matmul(psN[:, :], ones4[:, :], sqeo[:, 0:512],
                              start=True, stop=False)
                sb.inc("p2n", tensor.matmul(
                    psN[:, :], ones4[:, :], sqeo[:, 512:1024],
                    start=False, stop=True))
            tensor.wait_ge(sb.h("w2c"), 24)
            for tc in range(NT):
                tensor.wait_ge(sb.h("v2d%d" % (tc % 2)), 16 * (tc // 2 + 1))
                if tc >= 2:
                    tensor.wait_ge(sb.h("d2v"), tc - 1)
                lhs = lnvb[:, (tc % 2) * DM:(tc % 2) * DM + DM]
                last = None
                for bank in range(2):
                    for dc in range(NDC):
                        o = dc * 1024 + bank * 512
                        last = tensor.matmul(
                            psV[tc % 2][bank][:, :],
                            lhs[:, dc * 128:(dc + 1) * 128],
                            wqk_bf[:, o:o + 512],
                            start=(dc == 0), stop=(dc == NDC - 1))
                sb.inc("p2v", last)
            tensor.wait_ge(sb.h("nbx"), 2)
            sb.inc("nbx", tensor.transpose(psTb[0:1, 0:128], nbg[:, :],
                                           ident_b[:, :]))
            tensor.wait_ge(sb.h("nbx"), 5)
            sb.inc("nbx", tensor.matmul(psN[:, 0:1], ones_b[0:1, :],
                                        nbb[0:1, 0:1], start=True, stop=True))

        def w2cast(vector, idx):
            vector.wait_ge(sb.h("w2d%d" % (idx % 2)), 16 * (idx // 2 + 1))
            if idx >= 16:
                vector.wait_ge(sb.h("p2m"), 48)
            elif idx >= 8:
                vector.wait_ge(sb.h("p2m"), 16)
            dc = idx % 8
            sl = (idx % 2) * 1024
            sb.inc("w2c", vector.tensor_copy(
                wqk_bf[:, dc * 1024:(dc + 1) * 1024], wstg[:, sl:sl + 1024]))

        @blk.vector
        def _(vector):
            vector.wait_ge(sb.h("lo0"), 16 * (NT // 2))
            vector.wait_ge(sb.h("lo1"), 16 * (NT // 2))
            for (sec, ts, c) in QK_ITERS:
                g2 = giter(sec, ts, c)
                if g2 == 16:
                    for idx in range(8, 16):
                        w2cast(vector, idx)
                gt = gts(sec, ts)
                vector.wait_ge(sb.h("p2m"), g2 + 1)
                sl = gt % 2
                vector.wait_ge(sb.h("c2d%d" % sl), 32 * (gt // 2 + 1))
                psE = psEs[g2 % 2]
                psO = psOs[g2 % 2]
                cs = csb[:, sl * 512:sl * 512 + 512]
                sn = snb[:, sl * 512:sl * 512 + 512]
                bE = bqkv_pt[:, sec * 8 + c:sec * 8 + c + 1]
                bO = bqkv_pt[:, sec * 8 + 4 + c:sec * 8 + 4 + c + 1]
                if sec == 0:
                    sl2 = (g2 % 2) * 1024
                    oE = qout[:, sl2:sl2 + 512]
                    oO = qout[:, sl2 + 512:sl2 + 1024]
                else:
                    oE = k_fm[:, c * S + ts * 512:c * S + ts * 512 + 512]
                    oO = k_fm[:, (4 + c) * S + ts * 512:
                              (4 + c) * S + ts * 512 + 512]
                if g2 >= 1:
                    vector.wait_ge(sb.h("p2c"), g2)
                vector.scalar_tensor_tensor(
                    scrm[:, 0:512], psE[:, :], bE, cs, ALU.add, ALU.mult)
                sb.inc("r2e", vector.scalar_tensor_tensor(
                    scrm[:, 512:1024], psO[:, :], bO, sn, ALU.add, ALU.mult))
                if sec == 0 and g2 >= 2:
                    vector.wait_ge(sb.h("q2o" if g2 % 2 == 0 else "q2o1"),
                                   32 * (g2 // 2))
                    vector.wait_ge(sb.h("a2q"), g2 - 1)
                vector.wait_ge(sb.h("r2e"), g2 + 1)
                sb.inc("r2s", vector.tensor_sub(
                    oE, scrm[:, 0:512], scrm[:, 512:1024]))
                vector.wait_ge(sb.h("r2s"), g2 + 1)
                vector.scalar_tensor_tensor(
                    scrm[:, 0:512], psE[:, :], bE, sn, ALU.add, ALU.mult)
                sb.inc("d2m", vector.scalar_tensor_tensor(
                    scrm[:, 512:1024], psO[:, :], bO, cs, ALU.add, ALU.mult))
                vector.wait_ge(sb.h("d2m"), g2 + 1)
                sb.inc("p2c", vector.tensor_add(
                    oO, scrm[:, 0:512], scrm[:, 512:1024]))
                if g2 >= 2:
                    pidx = g2 - 2
                    vector.wait_ge(sb.h("p2n"), pidx + 1)
                    sb.inc("d2x", vector.reduce_max(
                        normc[:, pidx:pidx + 1], psN[:, :], axis=AX.X))
            for pidx in (46, 47):
                vector.wait_ge(sb.h("p2n"), pidx + 1)
                sb.inc("d2x", vector.reduce_max(
                    normc[:, pidx:pidx + 1], psN[:, :], axis=AX.X))
            for idx in range(16, 24):
                w2cast(vector, idx)
            for tc in range(NT):
                vector.wait_ge(sb.h("p2v"), tc + 1)
                base = tc * 16 * VW
                dst0 = v_tm[:, base:base + 8 * VW].rearrange(
                    "p (h u) -> p h u", u=VW)[:, :, 0:64]
                dst1 = v_tm[:, base + 8 * VW:base + 16 * VW].rearrange(
                    "p (h u) -> p h u", u=VW)[:, :, 0:64]
                src0 = psV[tc % 2][0][:, :].rearrange("p (h u) -> p h u", u=64)
                src1 = psV[tc % 2][1][:, :].rearrange("p (h u) -> p h u", u=64)
                bvb0 = bvb_t[:, 0:512].rearrange("p (h u) -> p h u", u=64)
                bvb1 = bvb_t[:, 512:1024].rearrange("p (h u) -> p h u", u=64)
                vector.tensor_add(dst0, src0, bvb0)
                sb.inc("d2v", vector.tensor_add(dst1, src1, bvb1))
            vector.wait_ge(sb.h("d2x"), 48)
            sb.inc("nbx", vector.reduce_max(nbt[:, 0:1], normc[:, 0:48],
                                            axis=AX.X))
            vector.wait_ge(sb.h("nbx"), 3)
            sb.inc("nbx", vector.reduce_max(nbt[0:1, 2:3], psTb[0:1, 0:128],
                                            axis=AX.X))
            vector.wait_ge(sb.h("nbx"), 6)
            sb.inc("negB", vector.tensor_copy(negB_t[:, :], psN[:, 0:1]))

        @blk.scalar
        def _(scalar):
            for (sec, ts, c) in QK_ITERS:
                g2 = giter(sec, ts, c)
                scalar.wait_ge(sb.h("p2c"), g2 + 1)
                if g2 >= 1:
                    scalar.wait_ge(sb.h("p2n"), g2)
                if sec == 0:
                    sl2 = (g2 % 2) * 1024
                    iE = qout[:, sl2:sl2 + 512]
                    iO = qout[:, sl2 + 512:sl2 + 1024]
                else:
                    iE = k_fm[:, c * S + ts * 512:c * S + ts * 512 + 512]
                    iO = k_fm[:, (4 + c) * S + ts * 512:
                              (4 + c) * S + ts * 512 + 512]
                scalar.activation(sqeo[:, 0:512], iE, AF.Square)
                sb.inc("a2q", scalar.activation(sqeo[:, 512:1024], iO,
                                                AF.Square))
            scalar.wait_ge(sb.h("nbx"), 4)
            sb.inc("nbx", scalar.activation(nbb[0:1, 0:1], nbt[0:1, 2:3],
                                            AF.Copy, scale=-0.126))

    st_bvb.close()
    st_w2.close()

    if phases <= 2:
        _finish(nc, sb, [("q2o", 16 * 32), ("d2v", 32), ("negB", 1)])
        return nc

    st3 = ExitStack()
    attn_fm = st3.enter_context(
        nc.sbuf_tensor("attn_fm", [128, 8 * OWN], BF16, side="right"))
    with (
        nc.sbuf_tensor("ring", [128, 5 * 2048], BF16, side="right") as ring,
        nc.sbuf_tensor("scr_un", [128, 1024], F32, side="right") as scr_un,
        nc.sbuf_tensor("den_sb", [1, 2048], F32, side="right") as den_sb,
        nc.sbuf_tensor("wcf", [128, 512], F32, side="right") as wcf,
        nc.sbuf_tensor("wcb", [128, 512], BF16, side="right") as wcb,
        nc.psum_tensor("psc", [128, 2048], F32) as psc,
        nc.psum_tensor("pwvA", [128, 2048], F32) as pwvA,
        nc.Block() as blk,
    ):
        pwv = [pwvA[:, r * 512:(r + 1) * 512] for r in range(4)]

        @blk.sync
        def _(sync):
            sync.wait_ge(sb.h("q2o"), 16 * 16)
            sync.wait_ge(sb.h("q2o1"), 16 * 16)

            def q3dma(qc):
                sb.inc("q3d", sync.dma_start(
                    q_qc.rearrange("p (ch u) -> p ch u", u=512),
                    qs_r[:, :, qc * 512:(qc + 1) * 512]), 16)

            for qc in range(1, n_qc_run):
                sync.wait_ge(sb.h("s3"), NG * KC * qc)
                q3dma(qc)

        @blk.gpsimd
        def _(gpsimd):
            chunks = []
            for cc in range(NDC):
                for j in range(2):
                    chunks.append((
                        wo_d[cc * 128:(cc + 1) * 128, j * 512:(j + 1) * 512],
                        wob_s[:, cc * 1024 + j * 512:cc * 1024 + j * 512 + 512]))
            for hc in range(32):
                for j in range(2):
                    chunks.append((
                        w2_d[hc * 128:(hc + 1) * 128, j * 512:(j + 1) * 512],
                        w2b_s[:, hc * 1024 + j * 512:hc * 1024 + j * 512 + 512]))
            for dc in range(NDC):
                for h in range(8):
                    chunks.append((
                        w1_d[dc * 128:(dc + 1) * 128, h * 512:(h + 1) * 512],
                        w1b_s[:, h * 4096 + dc * 512:h * 4096 + dc * 512 + 512]))
            for i, (src, dst) in enumerate(chunks):
                if i >= 1:
                    gpsimd.wait_ge(sb.h("wcc"), i)
                sb.inc("wcd", gpsimd.dma_start(wcf[:, :], src), 16)
                gpsimd.wait_ge(sb.h("wcd"), 16 * (i + 1))
                if i >= 1:
                    gpsimd.wait_ge(sb.h("wco"), 16 * i)
                sb.inc("wcc", gpsimd.tensor_copy(wcb[:, :], wcf[:, :]))
                gpsimd.wait_ge(sb.h("wcc"), i + 1)
                sb.inc("wco", gpsimd.dma_start(dst, wcb[:, :]), 16)

        def sc_mm(tensor, r, g, kc, first):
            rr = slice(32 * r, 32 * r + 32)
            ch = g if first else 4 + g
            return tensor.matmul(
                psc[:, r * 512:(r + 1) * 512],
                k_fm[rr, ch * S + kc * 128:ch * S + kc * 128 + 128],
                q_qc[rr, ch * 512:(ch + 1) * 512],
                start=first, stop=not first, tile_position=(32 * r, 0))

        def bc_mm(tensor, r):
            if r % 2 == 0:
                return tensor.matmul(
                    pwv[r][0:64, 0:512], ones_f[0:1, 0:64],
                    den_sb[0:1, r * 512:r * 512 + 512],
                    start=True, stop=True)
            return tensor.matmul(
                pwv[r][64:128, 0:512], ones_f[0:1, 0:64],
                den_sb[0:1, r * 512:r * 512 + 512],
                start=True, stop=True, tile_position=(0, 64))

        @blk.tensor
        def _(tensor):
            for qc in range(n_qc_run):
                tensor.wait_ge(sb.h("q3d"), 16 * (qc + 1))
                for g in range(NG):
                    G = qc * NG + g
                    if G >= att_g:
                        break
                    for kc in range(KC):
                        n = G * KC + kc
                        if n >= 1:
                            tensor.wait_ge(sb.h("e3a"), n)
                        sc_mm(tensor, 0, g, kc, True)
                        sc_mm(tensor, 0, g, kc, False)
                        sc_mm(tensor, 1, g, kc, True)
                        last = sc_mm(tensor, 1, g, kc, False)
                        sb.inc("s3a", last)
                        if n >= 1:
                            tensor.wait_ge(sb.h("e3b"), n)
                        sc_mm(tensor, 2, g, kc, True)
                        sc_mm(tensor, 2, g, kc, False)
                        sc_mm(tensor, 3, g, kc, True)
                        last = sc_mm(tensor, 3, g, kc, False)
                        sb.inc("s3", last)
                        if kc == 3 and G >= 1:
                            tensor.wait_ge(sb.h("ev3"), 2 * G - 1)
                            tensor.wait_ge(sb.h("dnd"), 2 * G)
                            sb.inc("b3", bc_mm(tensor, 0))
                            sb.inc("b3", bc_mm(tensor, 1))
                            tensor.wait_ge(sb.h("ev3"), 2 * G)
                            sb.inc("b3", bc_mm(tensor, 2))
                            sb.inc("b3", bc_mm(tensor, 3))
                        if kc >= 4:
                            kcw = kc - 4
                            nw = G * KC + kcw
                            tensor.wait_ge(sb.h("e3b"), nw + 1)
                            if kcw == 0 and G >= 1:
                                tensor.wait_ge(sb.h("n3"), 2 * G)
                            slot = (nw % 5) * 2048
                            wlast = None
                            for r in range(4):
                                h = 4 * g + r
                                wlast = tensor.matmul(
                                    pwv[r][0:VW, 0:512],
                                    v_tm[:, kcw * 16 * VW + h * VW:
                                         kcw * 16 * VW + h * VW + VW],
                                    ring[:, slot + r * 512:slot + (r + 1) * 512],
                                    start=(kcw == 0), stop=False)
                            sb.inc("w3", wlast)
                    for kcw in (KC - 4, KC - 3, KC - 2, KC - 1):
                        nw = G * KC + kcw
                        tensor.wait_ge(sb.h("e3b"), nw + 1)
                        slot = (nw % 5) * 2048
                        wlast = None
                        for r in range(4):
                            h = 4 * g + r
                            wlast = tensor.matmul(
                                pwv[r][0:VW, 0:512],
                                v_tm[:, kcw * 16 * VW + h * VW:
                                     kcw * 16 * VW + h * VW + VW],
                                ring[:, slot + r * 512:slot + (r + 1) * 512],
                                start=False, stop=(kcw == KC - 1))
                        sb.inc("w3", wlast)
            G = att_g - 1
            tensor.wait_ge(sb.h("ev3"), 2 * G + 1)
            tensor.wait_ge(sb.h("dnd"), 2 * G + 2)
            sb.inc("b3", bc_mm(tensor, 0))
            sb.inc("b3", bc_mm(tensor, 1))
            tensor.wait_ge(sb.h("ev3"), 2 * G + 2)
            sb.inc("b3", bc_mm(tensor, 2))
            sb.inc("b3", bc_mm(tensor, 3))

        def den_copies(scalar, Gp):
            scalar.wait_ge(sb.h("w3"), KC * (Gp + 1))
            scalar.wait_ge(sb.h("b3"), 4 * Gp)
            sb.inc("dnd", scalar.activation(den_sb[0:1, :],
                                            pwvA[64:65, 0:2048], AF.Ln))
            scalar.wait_ge(sb.h("dnd"), 2 * Gp + 1)
            sb.inc("dnd", scalar.activation(den_sb[0:1, :], den_sb[0:1, :],
                                            AF.Exp, scale=-1.0))

        @blk.scalar
        def _(scalar):
            scalar.wait_ge(sb.h("negB"), 1)
            for n in range(att_g * KC):
                slot = (n % 5) * 2048
                if n % KC == 2 and n >= KC:
                    den_copies(scalar, n // KC - 1)
                scalar.wait_ge(sb.h("s3a"), n + 1)
                if n >= 5:
                    scalar.wait_ge(sb.h("w3"), n - 4)
                sb.inc("e3a", scalar.activation(
                    ring[:, slot:slot + 1024], psc[:, 0:1024], AF.Exp,
                    bias=negB_t[:, :], scale=0.125))
                scalar.wait_ge(sb.h("s3"), n + 1)
                sb.inc("e3b", scalar.activation(
                    ring[:, slot + 1024:slot + 2048], psc[:, 1024:2048], AF.Exp,
                    bias=negB_t[:, :], scale=0.125))
            den_copies(scalar, att_g - 1)

        @blk.vector
        def _(vector):
            for G in range(att_g):
                qc, g = G // NG, G % NG
                vector.wait_ge(sb.h("w3"), KC * (G + 1))
                for half in range(2):
                    r0 = 2 * half
                    if half == 1:
                        vector.wait_ge(sb.h("n3"), 2 * G + 1)
                    vector.tensor_copy(scr_un[0:64, 0:512],
                                       pwv[r0][0:64, 0:512])
                    sb.inc("ev3", vector.tensor_copy(
                        scr_un[64:128, 512:1024], pwv[r0 + 1][0:64, 0:512]))
                    vector.wait_ge(sb.h("b3"), 4 * G + 2 * (half + 1))
                    last = None
                    for j in range(2):
                        r = r0 + j
                        h = 4 * g + r
                        dst = attn_fm[(h % 2) * 64:(h % 2) * 64 + 64,
                                      (h // 2) * OWN + qc * 512:
                                      (h // 2) * OWN + qc * 512 + 512]
                        rows = slice(0, 64) if j == 0 else slice(64, 128)
                        last = vector.tensor_tensor(
                            dst, scr_un[rows, j * 512:(j + 1) * 512],
                            pwv[r][rows, 0:512], ALU.mult)
                    sb.inc("n3", last)

    st_kv.close()

    st_bias = ExitStack()
    bo_t = st_bias.enter_context(nc.sbuf_tensor("bo_s", [128, DM], F32, side="right"))
    gam_t = st_bias.enter_context(nc.sbuf_tensor("gam_s", [128, DM], F32, side="right"))
    bet_t = st_bias.enter_context(nc.sbuf_tensor("bet_s", [128, DM], F32, side="right"))
    st_w5 = ExitStack()
    w1bf = st_w5.enter_context(
        nc.sbuf_tensor("w1bf", [128, 2 * 4096], BF16, side="left"))
    w2bf = st_w5.enter_context(
        nc.sbuf_tensor("w2bf", [128, 32 * 1024], BF16, side="left"))

    if phases <= 3:
        _finish(nc, sb, [("n3", 2 * att_g)])
        return nc

    with (
        nc.sbuf_tensor("wo_bf", [128, 8192], BF16, side="right") as wo_bf,
        nc.sbuf_tensor("xt4", [128, 2 * DM], F32, side="right") as xt4,
        nc.sbuf_tensor("scr_a", [128, DM], F32, side="right") as scr_a,
        nc.sbuf_tensor("scr_h", [128, DM], F32, side="right") as scr_h,
        nc.sbuf_tensor("scr_t", [128, DM], F32, side="right") as scr_t,
        nc.sbuf_tensor("scr_b", [128, DM], F32, side="right") as scr_b,
        nc.sbuf_tensor("x1b", [128, 2 * DM], F32, side="right") as x1b,
        nc.sbuf_tensor("lnbf3", [128, 2 * DM], BF16, side="right") as lnbf3,
        nc.sbuf_tensor("stg3", [128, 2 * DM], BF16, side="right") as stg3,
        nc.psum_tensor("pwoa", [128, 1024], F32) as pwoa,
        nc.psum_tensor("pwob", [128, 1024], F32) as pwob,
        nc.psum_tensor("pst4a", [128, 1024], BF16) as pst4a,
        nc.psum_tensor("pst4b", [128, 1024], BF16) as pst4b,
        nc.psum_tensor("jnk4", [128, 1024], F32) as junk,
        nc.Block() as blk,
    ):
        pwo = [pwoa, pwob]
        pst4 = [pst4a, pst4b]
        x4 = lambda t: xt4[:, (t % 2) * DM:(t % 2) * DM + DM]
        x1t = lambda t: x1b[:, (t % 2) * DM:(t % 2) * DM + DM]
        l3 = lambda t: lnbf3[:, (t % 2) * DM:(t % 2) * DM + DM]
        g3 = lambda t: stg3[:, (t % 2) * DM:(t % 2) * DM + DM]

        @blk.sync
        def _(sync):
            sync.wait_ge(sb.h("n3"), 2 * NQC * NG)
            sync.wait_ge(sb.h("wco"), 16 * 144)
            for src, dst in [(bo_d, bo_t), (gam_d, gam_t), (bet_d, bet_t)]:
                sb.inc("P0d", sync.dma_start(dst[:, :], src), 16)
            for t in range(NTO):
                if t >= 2:
                    sync.wait_ge(sb.h("p4x"), t - 1)
                sb.inc("x4d%d" % (t % 2), sync.dma_start(
                    x4(t), x_d[t * 128:(t + 1) * 128, :]), 16)
                if t >= 1:
                    sync.wait_ge(sb.h("p4x"), t)
                    sb.inc("sx1%d" % ((t - 1) % 2), sync.dma_start(
                        x1_s[:, (t - 1) * DM:t * DM], x1t(t - 1)), 16)
                if t >= 2:
                    sync.wait_ge(sb.h("a4V"), t - 1)
                    sb.inc("s3o%d" % ((t - 2) % 2), sync.dma_start(
                        ln3s_r[:, :, (t - 2) * 128:(t - 1) * 128],
                        g3(t - 2).rearrange("p (dc u) -> p dc u", u=128)), 16)
            sync.wait_ge(sb.h("p4x"), NTO)
            sb.inc("sx1%d" % ((NTO - 1) % 2), sync.dma_start(
                x1_s[:, (NTO - 1) * DM:NTO * DM], x1t(NTO - 1)), 16)
            for tp in (NTO - 2, NTO - 1):
                sync.wait_ge(sb.h("a4V"), tp + 1)
                sb.inc("s3o%d" % (tp % 2), sync.dma_start(
                    ln3s_r[:, :, tp * 128:(tp + 1) * 128],
                    g3(tp).rearrange("p (dc u) -> p dc u", u=128)), 16)

        @blk.gpsimd
        def _(gpsimd):
            gpsimd.wait_ge(sb.h("P0d"), sb.n("P0d"))
            gpsimd.wait_ge(sb.h("n3"), 2 * NQC * NG)
            gpsimd.wait_ge(sb.h("wco"), 16 * 144)
            for half in range(2):
                sb.inc("w4d", gpsimd.dma_start(
                    wo_bf[:, half * 4096:(half + 1) * 4096],
                    wob_s[:, half * 4096:(half + 1) * 4096]), 16)
            for t in range(NTO):
                gpsimd.wait_ge(sb.h("d4u"), t + 1)
                sb.inc("p4b", gpsimd.tensor_add(scr_b[:, :], scr_a[:, :],
                                                bet_t[:, :]))
                gpsimd.wait_ge(sb.h("p4b"), t + 1)
                if t >= 2:
                    gpsimd.wait_ge(sb.h("sx1%d" % (t % 2)), 16 * (t // 2))
                    gpsimd.wait_ge(sb.h("d4n"), t - 1)
                    gpsimd.wait_ge(sb.h("a4q3"), t - 1)
                sb.inc("p4x", gpsimd.tensor_add(x1t(t), scr_b[:, :], x4(t)))
            gpsimd.wait_ge(sb.h("wco"), 16 * 80)
            for j in range(8):
                sb.inc("w52d", gpsimd.dma_start(
                    w2bf[:, j * 4096:(j + 1) * 4096],
                    w2b_s[:, j * 4096:(j + 1) * 4096]), 16)
            gpsimd.wait_ge(sb.h("wco"), 16 * 144)
            for gg in range(2):
                sb.inc("w51d", gpsimd.dma_start(
                    w1bf[:, gg * 4096:(gg + 1) * 4096],
                    w1b_s[:, gg * 4096:(gg + 1) * 4096]), 16)

        @blk.tensor
        def _(tensor):
            tensor.wait_ge(sb.h("w4d"), 32)
            tensor.wait_ge(sb.h("n3"), 2 * NQC * NG)
            for t in range(NTO):
                if t >= 2:
                    tensor.wait_ge(sb.h("d4h"), 2 * (t - 1))
                last = None
                for bank in range(2):
                    for cc in range(NDC):
                        o = cc * 1024 + bank * 512
                        last = tensor.matmul(
                            pwo[t % 2][:, bank * 512:bank * 512 + 512],
                            attn_fm[:, cc * OWN + t * 128:cc * OWN + t * 128 + 128],
                            wo_bf[:, o:o + 512],
                            start=(cc == 0), stop=(cc == NDC - 1))
                sb.inc("p4o", last)
                if t >= 1:
                    tp = t - 1
                    tensor.wait_ge(sb.h("d4n"), tp + 1)
                    if tp >= 2:
                        tensor.wait_ge(sb.h("a4V"), tp - 1)
                    tlast = None
                    for dc in range(NDC):
                        tlast = tensor.transpose(
                            pst4[tp % 2][:, dc * 128:(dc + 1) * 128],
                            l3(tp)[:, dc * 128:(dc + 1) * 128], ident_b[:, :])
                    sb.inc("p4T", tlast)
            tp = NTO - 1
            tensor.wait_ge(sb.h("d4n"), tp + 1)
            tensor.wait_ge(sb.h("a4V"), tp - 1)
            tlast = None
            for dc in range(NDC):
                tlast = tensor.transpose(
                    pst4[tp % 2][:, dc * 128:(dc + 1) * 128],
                    l3(tp)[:, dc * 128:(dc + 1) * 128], ident_b[:, :])
            sb.inc("p4T", tlast)

        @blk.vector
        def _(vector):
            vector.wait_ge(sb.h("P0d"), sb.n("P0d"))
            vector.wait_ge(sb.h("n3"), 2 * NQC * NG)
            vector.wait_ge(sb.h("wco"), 16 * 144)
            for t in range(NTO):
                vector.wait_ge(sb.h("x4d%d" % (t % 2)), 16 * (t // 2 + 1))
                if t >= 1:
                    vector.wait_ge(sb.h("p4b"), t)
                sb.inc("d4a", vector.tensor_scalar(
                    scr_a[:, :], x4(t), mean1[:, t:t + 1], rstd1[:, t:t + 1],
                    ALU.subtract, ALU.mult))
                vector.wait_ge(sb.h("p4o"), t + 1)
                vector.wait_ge(sb.h("d4a"), t + 1)
                if t >= 1:
                    vector.wait_ge(sb.h("a4q2"), t)
                sb.inc("d4h", vector.tensor_add(scr_h[:, :], pwo[t % 2][:, :],
                                                scr_a[:, :]))
                vector.wait_ge(sb.h("d4h"), 2 * t + 1)
                sb.inc("d4h", vector.tensor_add(scr_h[:, :], scr_h[:, :],
                                                bo_t[:, :]))
                vector.wait_ge(sb.h("d4h"), 2 * t + 2)
                sb.inc("d4r2", vector.reduce_sum(s1b[:, t:t + 1], scr_h[:, :],
                                                 axis=AX.X))
                vector.wait_ge(sb.h("p4s2"), t + 1)
                vector.wait_ge(sb.h("a4q2"), t + 1)
                sb.inc("d4v2", vector.tensor_scalar(
                    vr2[:, t:t + 1], s2b[:, t:t + 1], t1b[:, t:t + 1], INV_DM,
                    ALU.subtract, ALU.mult))
                vector.wait_ge(sb.h("a4e2"), 2 * (t + 1))
                sb.inc("d4t", vector.tensor_scalar(
                    scr_t[:, :], scr_h[:, :], mn2[:, t:t + 1], rs2[:, t:t + 1],
                    ALU.subtract, ALU.mult))
                vector.wait_ge(sb.h("d4t"), t + 1)
                sb.inc("d4u", vector.tensor_mul(scr_a[:, :], scr_t[:, :],
                                                gam_t[:, :]))
                vector.wait_ge(sb.h("p4x"), t + 1)
                sb.inc("d4r3", vector.reduce_sum(s1d[:, t:t + 1], x1t(t),
                                                 axis=AX.X))
                vector.wait_ge(sb.h("p4s3"), t + 1)
                vector.wait_ge(sb.h("a4q3"), t + 1)
                sb.inc("d4v3", vector.tensor_scalar(
                    vr3[:, t:t + 1], s2d[:, t:t + 1], t1d[:, t:t + 1], INV_DM,
                    ALU.subtract, ALU.mult))
                vector.wait_ge(sb.h("a4e3"), 2 * (t + 1))
                if t >= 2:
                    vector.wait_ge(sb.h("p4T"), t - 1)
                sb.inc("d4n", vector.tensor_scalar(
                    l3(t), x1t(t), mn3[:, t:t + 1], rs3[:, t:t + 1],
                    ALU.subtract, ALU.mult))

        @blk.scalar
        def _(scalar):
            for t in range(NTO):
                scalar.wait_ge(sb.h("d4h"), 2 * t + 2)
                sb.inc("a4q2", scalar.activation(
                    junk[:, :], scr_h[:, :], AF.Square,
                    accum_out=s2b[:, t:t + 1]))
                scalar.wait_ge(sb.h("d4r2"), t + 1)
                scalar.activation(mn2[:, t:t + 1], s1b[:, t:t + 1], AF.Copy,
                                  scale=INV_DM)
                sb.inc("p4s2", scalar.activation(
                    t1b[:, t:t + 1], s1b[:, t:t + 1], AF.Square, scale=0.03125))
                scalar.wait_ge(sb.h("d4v2"), t + 1)
                sb.inc("a4e2", scalar.activation(
                    lv2[:, t:t + 1], vr2[:, t:t + 1], AF.Ln, bias=eps_t[:, :]))
                scalar.wait_ge(sb.h("a4e2"), 2 * t + 1)
                sb.inc("a4e2", scalar.activation(
                    rs2[:, t:t + 1], lv2[:, t:t + 1], AF.Exp, scale=-0.5))
                scalar.wait_ge(sb.h("p4x"), t + 1)
                sb.inc("a4q3", scalar.activation(
                    junk[:, :], x1t(t), AF.Square, accum_out=s2d[:, t:t + 1]))
                scalar.wait_ge(sb.h("d4r3"), t + 1)
                scalar.activation(mn3[:, t:t + 1], s1d[:, t:t + 1], AF.Copy,
                                  scale=INV_DM)
                sb.inc("p4s3", scalar.activation(
                    t1d[:, t:t + 1], s1d[:, t:t + 1], AF.Square, scale=0.03125))
                scalar.wait_ge(sb.h("d4v3"), t + 1)
                sb.inc("a4e3", scalar.activation(
                    lv3[:, t:t + 1], vr3[:, t:t + 1], AF.Ln, bias=eps_t[:, :]))
                scalar.wait_ge(sb.h("a4e3"), 2 * t + 1)
                sb.inc("a4e3", scalar.activation(
                    rs3[:, t:t + 1], lv3[:, t:t + 1], AF.Exp, scale=-0.5))
                if t >= 1:
                    tp = t - 1
                    scalar.wait_ge(sb.h("p4T"), tp + 1)
                    if tp >= 2:
                        scalar.wait_ge(sb.h("s3o%d" % (tp % 2)), 16 * (tp // 2))
                    sb.inc("a4V", scalar.copy(g3(tp), pst4[tp % 2][:, :]))
            tp = NTO - 1
            scalar.wait_ge(sb.h("p4T"), tp + 1)
            scalar.wait_ge(sb.h("s3o%d" % (tp % 2)), 16 * (tp // 2))
            sb.inc("a4V", scalar.copy(g3(tp), pst4[tp % 2][:, :]))

    st_bias.close()
    st3.close()

    if phases <= 4:
        _finish(nc, sb, [("s3o0", 16 * 8), ("s3o1", 16 * 8),
                         ("sx10", 16 * 8), ("sx11", 16 * 8)])
        return nc

    with (
        nc.sbuf_tensor("ln3b5", [128, 2 * 4096], BF16, side="right") as ln3b5,
        nc.sbuf_tensor("h1", [128, 2 * 16384], BF16, side="right") as h1,
        nc.sbuf_tensor("x1b5", [128, 2 * DM], F32, side="right") as x1b5,
        nc.sbuf_tensor("scr5", [128, 2 * DM], F32, side="right") as scr5,
        nc.sbuf_tensor("b2_s", [128, DM], F32, side="right") as b2_t,
        nc.psum_tensor("pfA", [128, 1024], F32) as pfA,
        nc.psum_tensor("pfB", [128, 1024], F32) as pfB,
        nc.psum_tensor("pfC", [128, 1024], F32) as pfC,
        nc.psum_tensor("pfD", [128, 1024], F32) as pfD,
        nc.Block() as blk,
    ):
        pf2 = [pfC, pfD]

        @blk.gpsimd
        def _(gpsimd):
            for gg in range(2, 32):
                gpsimd.wait_ge(sb.h("p5f1"), 4 * (gg - 1))
                gpsimd.wait_ge(sb.h("w51d"), 16 * gg)
                sb.inc("w51d", gpsimd.dma_start(
                    w1bf[:, (gg % 2) * 4096:(gg % 2) * 4096 + 4096],
                    w1b_s[:, (gg % 8) * 4096:(gg % 8) * 4096 + 4096]), 16)

        @blk.sync
        def _(sync):
            for nm in ("s3o0", "s3o1", "sx10", "sx11"):
                sync.wait_ge(sb.h(nm), 16 * (NTO // 2))
            sb.inc("P0d", sync.dma_start(b2_t[:, :], b2_d), 16)

            def out_dma(sync, tt):
                sync.wait_ge(sb.h("d5e"), 2 * tt + 2)
                sb.inc("o5%d" % (tt % 2), sync.dma_start(
                    out_d[tt * 128:(tt + 1) * 128, :],
                    x1b5[:, (tt % 2) * DM:(tt % 2) * DM + DM]), 16)

            def ln3_dma(sync, bb):
                if bb >= 2:
                    sync.wait_ge(sb.h("p5f1"), 32 * (bb - 1))
                sb.inc("l5d" if bb % 2 == 0 else "l5d1", sync.dma_start(
                    ln3b5[:, (bb % 2) * 4096:(bb % 2) * 4096 + 4096].rearrange(
                        "p (dc u) -> p dc u", u=512),
                    ln3s_r[:, :, bb * 512:(bb + 1) * 512]), 16)

            ln3_dma(sync, 0)
            for b in range(4):
                if b + 1 < 4:
                    ln3_dma(sync, b + 1)
                for tt in range(b * 4, b * 4 + 4):
                    if tt >= 2:
                        out_dma(sync, tt - 2)
                        sync.wait_ge(sb.h("d5e"), 2 * tt - 2)
                        sync.wait_ge(sb.h("o5%d" % (tt % 2)), 16 * (tt // 2))
                    sb.inc("x5d%d" % (tt % 2), sync.dma_start(
                        x1b5[:, (tt % 2) * DM:(tt % 2) * DM + DM],
                        x1_s[:, tt * DM:(tt + 1) * DM]), 16)
            out_dma(sync, 14)
            out_dma(sync, 15)

        @blk.vector
        def _(vector):
            for nm in ("s3o0", "s3o1", "sx10", "sx11"):
                vector.wait_ge(sb.h(nm), 16 * (NTO // 2))
            vector.wait_ge(sb.h("P0d"), sb.n("P0d"))

            def evacs(vector, bb):
                for p in range(2):
                    for tt in range(bb * 4 + p * 2, bb * 4 + p * 2 + 2):
                        tcl = tt - (bb * 4 + p * 2)
                        vector.wait_ge(sb.h("p5f2"), 32 * (bb * 2 + p + 1))
                        vector.wait_ge(sb.h("x5d%d" % (tt % 2)),
                                       16 * (tt // 2 + 1))
                        if tt >= 2:
                            vector.wait_ge(sb.h("d5e"), 2 * tt - 2)
                        sb.inc("d5e", vector.tensor_add(
                            scr5[:, (tt % 2) * DM:(tt % 2) * DM + DM],
                            pf2[tcl][:, :],
                            x1b5[:, (tt % 2) * DM:(tt % 2) * DM + DM]))
                        vector.wait_ge(sb.h("d5e"), 2 * tt + 1)
                        sb.inc("d5e", vector.tensor_add(
                            x1b5[:, (tt % 2) * DM:(tt % 2) * DM + DM],
                            scr5[:, (tt % 2) * DM:(tt % 2) * DM + DM],
                            b2_t[:, :]))

            for b in range(4):
                evacs(vector, b)

        @blk.tensor
        def _(tensor):
            pf1banks = [pfA[:, 0:512], pfA[:, 512:1024],
                        pfB[:, 0:512], pfB[:, 512:1024]]
            tensor.wait_ge(sb.h("w52d"), 16 * 8)
            for b in range(4):
                tensor.wait_ge(sb.h("l5d" if b % 2 == 0 else "l5d1"),
                               16 * (b // 2 + 1))
                for hc in range(32):
                    gg = b * 8 + hc // 4
                    tensor.wait_ge(sb.h("w51d"),
                                   32 if gg < 2 else 16 * (gg + 1))
                    if b * 32 + hc >= 4:
                        tensor.wait_ge(sb.h("a5g"), b * 32 + hc - 3)
                    bank = pf1banks[hc % 4]
                    last = None
                    for dc in range(NDC):
                        o = (gg % 2) * 4096 + dc * 512 + (hc % 4) * 128
                        last = tensor.matmul(
                            bank, w1bf[:, o:o + 128],
                            ln3b5[:, (b % 2) * 4096 + dc * 512:
                                  (b % 2) * 4096 + (dc + 1) * 512],
                            start=(dc == 0), stop=(dc == NDC - 1))
                    sb.inc("p5f1", last)
                for p in range(2):
                    for hc in range(32):
                        tensor.wait_ge(sb.h("a5g"), b * 32 + hc + 1)
                        if hc == 0 and b * 2 + p >= 1:
                            prev_tt = (b * 2 + p) * 2 - 1
                            tensor.wait_ge(sb.h("d5e"), 2 * prev_tt + 1)
                        last = None
                        for tcl in range(2):
                            tok = (p * 2 + tcl) * 128
                            for bank in range(2):
                                last = tensor.matmul(
                                    pf2[tcl][:, bank * 512:bank * 512 + 512],
                                    h1[:, (b % 2) * 16384 + hc * 512 + tok:
                                       (b % 2) * 16384 + hc * 512 + tok + 128],
                                    w2bf[:, hc * 1024 + bank * 512:
                                         hc * 1024 + bank * 512 + 512],
                                    start=(hc == 0), stop=(hc == 31))
                        sb.inc("p5f2", last)

        @blk.scalar
        def _(scalar):
            for b in range(4):
                for hc in range(32):
                    scalar.wait_ge(sb.h("p5f1"), b * 32 + hc + 1)
                    if hc == 0 and b >= 2:
                        scalar.wait_ge(sb.h("p5f2"), 64 * (b - 1))
                    sb.inc("a5g", scalar.activation(
                        h1[:, (b % 2) * 16384 + hc * 512:
                           (b % 2) * 16384 + (hc + 1) * 512],
                        (pfA if hc % 4 < 2 else pfB)[
                            :, (hc % 2) * 512:(hc % 2) * 512 + 512],
                        AF.Gelu, bias=b1_pt[:, hc:hc + 1]))

    with nc.Block() as blk:
        @blk.sync
        def _(sync):
            sync.wait_ge(sb.h("o50"), 16 * (NTO // 2))
            sync.wait_ge(sb.h("o51"), 16 * (NTO // 2))

    return nc


_CACHE = {}


def _host_prep(Wqkv, bqkv):
    perm = []
    for base in (0, DM):
        for eo in (0, 1):
            for c in range(4):
                for hh in range(4):
                    h = 4 * c + hh
                    for i in range(32):
                        perm.append(base + h * 64 + 2 * i + eo)
    perm += list(range(2 * DM, 3 * DM))
    perm = np.asarray(perm)
    Wp = np.ascontiguousarray(Wqkv[:, perm], dtype=np.float32)
    bp = np.ascontiguousarray(bqkv[perm], dtype=np.float32)
    return Wp, bp


def _tables(positions):
    i = np.arange(32, dtype=np.float64)
    inv = 10000.0 ** (-2.0 * i / 64.0)
    ang = positions[None, :].astype(np.float64) * inv[:, None]
    cs = np.tile(np.cos(ang), (4, 1)).astype(np.float32)
    sn = np.tile(np.sin(ang), (4, 1)).astype(np.float32)
    return np.ascontiguousarray(cs), np.ascontiguousarray(sn)


def kernel(L0_attn_x, L0_attn_Wqkv, L0_attn_bqkv, L0_attn_Wo, L0_attn_bo,
           L0_attn_gamma, L0_attn_beta, L0_ffn_w1, L0_ffn_b1, L0_ffn_w2,
           L0_ffn_b2):
    x = np.ascontiguousarray(np.asarray(L0_attn_x), dtype=np.float32)
    Wqkv = np.asarray(L0_attn_Wqkv, dtype=np.float32)
    bqkv = np.asarray(L0_attn_bqkv, dtype=np.float32)
    Wo = np.ascontiguousarray(np.asarray(L0_attn_Wo), dtype=np.float32)
    bo = np.asarray(L0_attn_bo, dtype=np.float32)
    gam = np.asarray(L0_attn_gamma, dtype=np.float32)
    bet = np.asarray(L0_attn_beta, dtype=np.float32)
    W1 = np.ascontiguousarray(np.asarray(L0_ffn_w1), dtype=np.float32)
    b1 = np.asarray(L0_ffn_b1, dtype=np.float32)
    W2 = np.ascontiguousarray(np.asarray(L0_ffn_w2), dtype=np.float32)
    b2 = np.asarray(L0_ffn_b2, dtype=np.float32)

    Wp, bp = _host_prep(Wqkv, bqkv)
    bqkv_pt = np.ascontiguousarray(bp[0:2 * DM].reshape(16, 128).T)
    bvb_t = np.ascontiguousarray(np.tile(bp[2 * DM:3 * DM], (128, 1)))
    bo_t = np.ascontiguousarray(np.tile(bo, (128, 1)))
    b2_t = np.ascontiguousarray(np.tile(b2, (128, 1)))
    gam_t = np.ascontiguousarray(np.tile(gam, (128, 1)))
    bet_t = np.ascontiguousarray(np.tile(bet, (128, 1)))
    b1_pt = np.ascontiguousarray(b1.reshape(32, 128).T)
    ident = np.eye(128, dtype=np.float32)

    in_maps = []
    for core in range(8):
        b, half = core // 2, core % 2
        if half == 0:
            xr = x[b]
            pos = np.arange(S)
        else:
            xr = np.concatenate([x[b, OWN:], x[b, :OWN]], axis=0)
            pos = np.concatenate([np.arange(OWN, S), np.arange(0, OWN)])
        csk, snk = _tables(pos)
        in_maps.append({
            "x": np.ascontiguousarray(xr),
            "wqkv": Wp, "wo": Wo, "w1": W1, "w2": W2,
            "bqkv_pt": bqkv_pt, "bvb_t": bvb_t, "bo_t": bo_t,
            "b1_pt": b1_pt, "b2_t": b2_t, "gam_t": gam_t, "bet_t": bet_t,
            "csq": np.ascontiguousarray(csk[:, 0:OWN]),
            "snq": np.ascontiguousarray(snk[:, 0:OWN]),
            "csk": csk, "snk": snk, "ident": ident,
        })

    if "nc" not in _CACHE:
        _CACHE["nc"] = build_nc()
    nc = _CACHE["nc"]
    res = run_bass_kernel_spmd(nc, in_maps, core_ids=list(range(8)))

    out = np.empty((4, S, DM), dtype=np.float32)
    for core in range(8):
        b, half = core // 2, core % 2
        out[b, half * OWN:(half + 1) * OWN] = res.results[core]["out"]
    return out


# version 25
# speedup vs baseline: 1.0260x; 1.0260x over previous
import numpy as np
from contextlib import ExitStack

import concourse.bass as bass
import concourse.mybir as mybir
from concourse.bass_utils import run_bass_kernel_spmd

F32 = mybir.dt.float32
BF16 = mybir.dt.bfloat16
AF = mybir.ActivationFunctionType
ALU = mybir.AluOpType
AX = mybir.AxisListType

S = 4096
OWN = 2048
DM = 1024
NH = 16
DH = 64
FF = 4096
NDC = DM // 128
NT = S // 128
NTO = OWN // 128
QC = 512
NQC = OWN // QC
NG = 4
KC = S // 128
VW = 65
INV_DM = 1.0 / DM


class Sems:
    def __init__(self, nc, st):
        self.nc = nc
        self.st = st
        self.m = {}

    def new(self, *names):
        for name in names:
            h = self.st.enter_context(self.nc.semaphore(name))
            self.m[name] = [h, 0]

    def inc(self, name, instr, k=1):
        e = self.m[name]
        instr.then_inc(e[0], k)
        e[1] += k
        return instr

    def h(self, name):
        return self.m[name][0]

    def n(self, name):
        return self.m[name][1]


def _finish(nc, sb, waits):
    with nc.Block() as blk:
        @blk.sync
        def _(sync):
            for nm, v in waits:
                sync.wait_ge(sb.h(nm), v)


def build_nc(phases=5, att_g=NQC * NG):
    nc = bass.Bass("TRN2", target_bir_lowering=False)
    n_qc_run = (att_g + NG - 1) // NG

    x_d = nc.dram_tensor("x", [S, DM], F32, kind="ExternalInput").ap()
    wqkv_d = nc.dram_tensor("wqkv", [DM, 3 * DM], F32, kind="ExternalInput").ap()
    wo_d = nc.dram_tensor("wo", [DM, DM], F32, kind="ExternalInput").ap()
    w1_d = nc.dram_tensor("w1", [DM, FF], F32, kind="ExternalInput").ap()
    w2_d = nc.dram_tensor("w2", [FF, DM], F32, kind="ExternalInput").ap()
    bqkvpt_d = nc.dram_tensor("bqkv_pt", [128, 16], F32, kind="ExternalInput").ap()
    bvb_d = nc.dram_tensor("bvb_t", [128, DM], F32, kind="ExternalInput").ap()
    bo_d = nc.dram_tensor("bo_t", [128, DM], F32, kind="ExternalInput").ap()
    b1pt_d = nc.dram_tensor("b1_pt", [128, 32], F32, kind="ExternalInput").ap()
    b2_d = nc.dram_tensor("b2_t", [128, DM], F32, kind="ExternalInput").ap()
    gam_d = nc.dram_tensor("gam_t", [128, DM], F32, kind="ExternalInput").ap()
    bet_d = nc.dram_tensor("bet_t", [128, DM], F32, kind="ExternalInput").ap()
    csq_d = nc.dram_tensor("csq", [128, OWN], F32, kind="ExternalInput").ap()
    snq_d = nc.dram_tensor("snq", [128, OWN], F32, kind="ExternalInput").ap()
    csk_d = nc.dram_tensor("csk", [128, S], F32, kind="ExternalInput").ap()
    snk_d = nc.dram_tensor("snk", [128, S], F32, kind="ExternalInput").ap()
    ident_d = nc.dram_tensor("ident", [128, 128], F32, kind="ExternalInput").ap()
    out_d = nc.dram_tensor("out", [OWN, DM], F32, kind="ExternalOutput").ap()

    ln1_s = nc.dram_tensor("ln1_s", [128, NDC * S], BF16, kind="Internal").ap()
    q_s = nc.dram_tensor("q_s", [128, 8 * OWN], BF16, kind="Internal").ap()
    x1_s = nc.dram_tensor("x1_s", [128, NTO * DM], F32, kind="Internal").ap()
    ln3_s = nc.dram_tensor("ln3_s", [128, NDC * OWN], BF16, kind="Internal").ap()
    wob_s = nc.dram_tensor("wob_s", [128, 8 * 1024], BF16, kind="Internal").ap()
    w2b_s = nc.dram_tensor("w2b_s", [128, 32 * 1024], BF16, kind="Internal").ap()
    w1b_s = nc.dram_tensor("w1b_s", [128, 8 * 4096], BF16, kind="Internal").ap()

    ln1s_r = ln1_s.rearrange("p (dc t) -> p dc t", t=S)
    qs_r = q_s.rearrange("p (ch t) -> p ch t", t=OWN)
    ln3s_r = ln3_s.rearrange("p (dc t) -> p dc t", t=OWN)
    w1bs_r = w1b_s.rearrange("p (h d u) -> p h d u", d=8, u=512)

    st = ExitStack()
    sb = Sems(nc, st)
    sb.new("P0d",
           "xd0", "xd1", "aS", "dR", "pM", "dV", "aL", "aE", "dN", "pT", "aV",
           "lo0", "lo1",
           "w2d0", "w2d1", "w2c", "l2d0", "l2d1", "c2d0", "c2d1", "p2m", "d2m", "p2c", "a2q", "p2n", "r2e", "r2s",
           "d2x", "q2o", "q2o1", "nbx", "negB", "v2d0", "v2d1", "p2v", "d2v",
           "q3d", "s3", "s3a", "e3a", "e3b", "w3", "ev3", "b3", "n3", "dnd",
           "wcd", "wcc", "wco",
           "x4d0", "x4d1", "w4d", "p4o", "d4a", "d4h", "a4q2", "d4r2",
           "p4s2", "d4v2", "a4e2", "d4t", "d4u", "p4b", "p4x", "sx10", "sx11",
           "a4q3", "d4r3", "p4s3", "d4v3", "a4e3", "d4n", "p4T", "a4V",
           "s3o0", "s3o1",
           "l5d", "l5d1", "w51d", "w52d", "p5f1", "a5g", "p5f2",
           "x5d0", "x5d1", "d5e", "p5o", "o50", "o51")

    L = dict(side="left")
    eps_t = st.enter_context(nc.sbuf_tensor("eps_t", [128, 1], F32, **L))
    negB_t = st.enter_context(nc.sbuf_tensor("negB_t", [128, 1], F32, **L))
    ones_f = st.enter_context(nc.sbuf_tensor("ones_f", [1, 128], F32, **L))
    ones_b = st.enter_context(nc.sbuf_tensor("ones_b", [1, 128], BF16, **L))
    ones4 = st.enter_context(nc.sbuf_tensor("ones4", [128, 128], BF16, **L))
    ident_b = st.enter_context(nc.sbuf_tensor("ident_b", [128, 128], BF16, **L))
    bqkv_pt = st.enter_context(nc.sbuf_tensor("bqkv_pt_s", [128, 16], F32, **L))
    b1_pt = st.enter_context(nc.sbuf_tensor("b1_pt_s", [128, 32], F32, **L))
    s1 = st.enter_context(nc.sbuf_tensor("s1", [128, NT], F32, **L))
    s2 = st.enter_context(nc.sbuf_tensor("s2", [128, NT], F32, **L))
    t1c = st.enter_context(nc.sbuf_tensor("t1c", [128, NT], F32, **L))
    mean1 = st.enter_context(nc.sbuf_tensor("mean1", [128, NT], F32, **L))
    var1 = st.enter_context(nc.sbuf_tensor("var1", [128, NT], F32, **L))
    lnv1 = st.enter_context(nc.sbuf_tensor("lnv1", [128, NT], F32, **L))
    rstd1 = st.enter_context(nc.sbuf_tensor("rstd1", [128, NT], F32, **L))
    s1b = st.enter_context(nc.sbuf_tensor("s1b", [128, NTO], F32, **L))
    s2b = st.enter_context(nc.sbuf_tensor("s2b", [128, NTO], F32, **L))
    t1b = st.enter_context(nc.sbuf_tensor("t1b", [128, NTO], F32, **L))
    mn2 = st.enter_context(nc.sbuf_tensor("mn2", [128, NTO], F32, **L))
    vr2 = st.enter_context(nc.sbuf_tensor("vr2", [128, NTO], F32, **L))
    lv2 = st.enter_context(nc.sbuf_tensor("lv2", [128, NTO], F32, **L))
    rs2 = st.enter_context(nc.sbuf_tensor("rs2", [128, NTO], F32, **L))
    s1d = st.enter_context(nc.sbuf_tensor("s1d", [128, NTO], F32, **L))
    s2d = st.enter_context(nc.sbuf_tensor("s2d", [128, NTO], F32, **L))
    t1d = st.enter_context(nc.sbuf_tensor("t1d", [128, NTO], F32, **L))
    mn3 = st.enter_context(nc.sbuf_tensor("mn3", [128, NTO], F32, **L))
    vr3 = st.enter_context(nc.sbuf_tensor("vr3", [128, NTO], F32, **L))
    lv3 = st.enter_context(nc.sbuf_tensor("lv3", [128, NTO], F32, **L))
    rs3 = st.enter_context(nc.sbuf_tensor("rs3", [128, NTO], F32, **L))
    normc = st.enter_context(nc.sbuf_tensor("normc", [128, 48], F32, **L))
    nbt = st.enter_context(nc.sbuf_tensor("nbt", [128, 4], F32, **L))
    nbg = st.enter_context(nc.sbuf_tensor("nbg", [128, 1], BF16, **L))
    nbb = st.enter_context(nc.sbuf_tensor("nbb", [1, 4], BF16, **L))
    st_kv = ExitStack()
    k_fm = st_kv.enter_context(nc.sbuf_tensor("k_fm", [128, 8 * S], BF16, **L))
    v_tm = st_kv.enter_context(nc.sbuf_tensor("v_tm", [128, KC * 16 * VW], BF16, **L))
    q_qc = st_kv.enter_context(nc.sbuf_tensor("q_qc", [128, 4096], BF16, **L))
    st_w2 = ExitStack()
    wqk_bf = st_w2.enter_context(
        nc.sbuf_tensor("wqk_bf", [128, 8192], BF16, side="right"))
    wstg = st_w2.enter_context(
        nc.sbuf_tensor("wstg", [128, 2048], F32, side="right"))

    st_bvb = ExitStack()
    bvb_t = st_bvb.enter_context(
        nc.sbuf_tensor("bvb_s", [128, DM], F32, side="right"))
    st_id = ExitStack()
    identf = st_id.enter_context(
        nc.sbuf_tensor("identf", [128, 128], F32, side="right"))

    with nc.Block() as blk:
        @blk.sync
        def _(sync):
            for src, dst in [(bqkvpt_d, bqkv_pt), (b1pt_d, b1_pt),
                             (bvb_d, bvb_t), (ident_d, identf)]:
                sb.inc("P0d", sync.dma_start(dst[:, :], src), 16)

        @blk.vector
        def _(vector):
            vector.wait_ge(sb.h("P0d"), sb.n("P0d"))
            vector.tensor_copy(ident_b[:, :], identf[:, :])
            vector.memset(eps_t[:, :], 1e-5)
            vector.memset(ones_f[:, :], 1.0)
            vector.memset(ones_b[:, :], 1.0)
            vector.memset(ones4[:, :], 0.0)
            vones = v_tm[:, :].rearrange("p (kc h u) -> p kc h u", h=16, u=VW)
            vector.memset(vones[:, :, :, 64:65], 1.0)

    with nc.Block() as blk:
        @blk.vector
        def _(vector):
            for j in range(4):
                vector.memset(ones4[32 * j:32 * (j + 1), 32 * j:32 * j + 1], 1.0)

    st_id.close()

    with (
        nc.sbuf_tensor("xt1", [128, 2 * DM], F32, side="right") as xt1,
        nc.sbuf_tensor("lnbf1", [128, 2 * DM], BF16, side="right") as lnbf1,
        nc.sbuf_tensor("stg1", [128, 2 * DM], BF16, side="right") as stg1,
        nc.psum_tensor("pst1a", [128, 1024], BF16) as pst1a,
        nc.psum_tensor("pst1b", [128, 1024], BF16) as pst1b,
        nc.psum_tensor("jnk1", [128, 1024], F32) as junk,
        nc.Block() as blk,
    ):
        pst1 = [pst1a, pst1b]
        xb = lambda t: xt1[:, (t % 2) * DM:(t % 2) * DM + DM]
        lb = lambda t: lnbf1[:, (t % 2) * DM:(t % 2) * DM + DM]
        gb = lambda t: stg1[:, (t % 2) * DM:(t % 2) * DM + DM]

        @blk.sync
        def _(sync):
            for t in range(NT):
                if t >= 2:
                    sync.wait_ge(sb.h("dN"), t - 1)
                sb.inc("xd%d" % (t % 2), sync.dma_start(
                    xb(t), x_d[t * 128:(t + 1) * 128, :]), 16)

        @blk.scalar
        def _(scalar):
            def evac(tp):
                scalar.wait_ge(sb.h("pT"), tp + 1)
                if tp >= 2:
                    scalar.wait_ge(sb.h("lo%d" % (tp % 2)), 16 * (tp // 2))
                sb.inc("aV", scalar.copy(gb(tp), pst1[tp % 2][:, :]))

            for t in range(NT):
                if t >= 2:
                    evac(t - 2)
                scalar.wait_ge(sb.h("xd%d" % (t % 2)), 16 * (t // 2 + 1))
                sb.inc("aS", scalar.activation(
                    junk[:, :], xb(t), AF.Square, accum_out=s2[:, t:t + 1]))
                scalar.wait_ge(sb.h("dR"), t + 1)
                scalar.activation(mean1[:, t:t + 1], s1[:, t:t + 1], AF.Copy,
                                  scale=INV_DM)
                sb.inc("pM", scalar.activation(
                    t1c[:, t:t + 1], s1[:, t:t + 1], AF.Square, scale=0.03125))
                scalar.wait_ge(sb.h("dV"), t + 1)
                sb.inc("aL", scalar.activation(
                    lnv1[:, t:t + 1], var1[:, t:t + 1], AF.Ln, bias=eps_t[:, :]))
                scalar.wait_ge(sb.h("aL"), t + 1)
                sb.inc("aE", scalar.activation(
                    rstd1[:, t:t + 1], lnv1[:, t:t + 1], AF.Exp, scale=-0.5))
            evac(NT - 2)
            evac(NT - 1)

        @blk.vector
        def _(vector):
            for t in range(NT):
                if 2 <= t < 10:
                    idx = t - 2
                    vector.wait_ge(sb.h("w2d%d" % (idx % 2)),
                                   16 * (idx // 2 + 1))
                    sb.inc("w2c", vector.tensor_copy(
                        wqk_bf[:, (idx % 8) * 1024:(idx % 8) * 1024 + 1024],
                        wstg[:, (idx % 2) * 1024:(idx % 2) * 1024 + 1024]))
                vector.wait_ge(sb.h("xd%d" % (t % 2)), 16 * (t // 2 + 1))
                sb.inc("dR", vector.reduce_sum(s1[:, t:t + 1], xb(t), axis=AX.X))
                vector.wait_ge(sb.h("pM"), t + 1)
                vector.wait_ge(sb.h("aS"), t + 1)
                sb.inc("dV", vector.tensor_scalar(
                    var1[:, t:t + 1], s2[:, t:t + 1], t1c[:, t:t + 1], INV_DM,
                    ALU.subtract, ALU.mult))
                vector.wait_ge(sb.h("aE"), t + 1)
                if t >= 2:
                    vector.wait_ge(sb.h("pT"), t - 1)
                sb.inc("dN", vector.tensor_scalar(
                    lb(t), xb(t), mean1[:, t:t + 1], rstd1[:, t:t + 1],
                    ALU.subtract, ALU.mult))

        @blk.gpsimd
        def _(gpsimd):
            for t in range(NT):
                if t < 8:
                    idx = t
                    if idx >= 2:
                        gpsimd.wait_ge(sb.h("w2c"), idx - 1)
                    sl = (idx % 2) * 1024
                    sb.inc("w2d%d" % (idx % 2), gpsimd.dma_start(
                        wstg[:, sl:sl + 1024],
                        wqkv_d[idx * 128:(idx + 1) * 128, 0:1024]), 16)
                if t >= 1:
                    ts_ = t - 1
                    gpsimd.wait_ge(sb.h("aV"), ts_ + 1)
                    sb.inc("lo%d" % (ts_ % 2), gpsimd.dma_start(
                        ln1s_r[:, :, ts_ * 128:(ts_ + 1) * 128],
                        gb(ts_).rearrange("p (dc u) -> p dc u", u=128)), 16)
            ts_ = NT - 1
            gpsimd.wait_ge(sb.h("aV"), ts_ + 1)
            sb.inc("lo%d" % (ts_ % 2), gpsimd.dma_start(
                ln1s_r[:, :, ts_ * 128:(ts_ + 1) * 128],
                gb(ts_).rearrange("p (dc u) -> p dc u", u=128)), 16)

        @blk.tensor
        def _(tensor):
            for t in range(NT):
                tensor.wait_ge(sb.h("dN"), t + 1)
                if t >= 2:
                    tensor.wait_ge(sb.h("aV"), t - 1)
                last = None
                for dc in range(NDC):
                    last = tensor.transpose(
                        pst1[t % 2][:, dc * 128:(dc + 1) * 128],
                        lb(t)[:, dc * 128:(dc + 1) * 128], ident_b[:, :])
                sb.inc("pT", last)

    if phases <= 1:
        _finish(nc, sb, [("lo0", 16 * 16), ("lo1", 16 * 16)])
        return nc

    QK_ITERS = [(0, ts, c) for ts in range(4) for c in range(4)] + \
               [(1, ts, c) for ts in range(8) for c in range(4)]

    def giter(sec, ts, c):
        return (ts * 4 + c) if sec == 0 else (16 + ts * 4 + c)

    def gts(sec, ts):
        return ts if sec == 0 else 4 + ts

    with (
        nc.sbuf_tensor("lnbuf2", [128, 2 * 4096], BF16, side="right") as lnbuf2,
        nc.sbuf_tensor("scrm", [128, 2 * 512], F32, side="right") as scrm,
        nc.sbuf_tensor("qout", [128, 2 * 1024], BF16, side="right") as qout,
        nc.sbuf_tensor("sqeo", [128, 1024], BF16, side="right") as sqeo,
        nc.sbuf_tensor("csb", [128, 2 * 512], F32, side="right") as csb,
        nc.sbuf_tensor("snb", [128, 2 * 512], F32, side="right") as snb,
        nc.sbuf_tensor("lnvb", [128, 2 * DM], BF16, side="right") as lnvb,
        nc.psum_tensor("psE0", [128, 512], F32) as psE0,
        nc.psum_tensor("psO0", [128, 512], F32) as psO0,
        nc.psum_tensor("psE1", [128, 512], F32) as psE1,
        nc.psum_tensor("psO1", [128, 512], F32) as psO1,
        nc.psum_tensor("psN", [128, 512], F32) as psN,
        nc.psum_tensor("psTb", [128, 128], BF16) as psTb,
        nc.Block() as blk,
    ):
        psEs = [psE0, psE1]
        psOs = [psO0, psO1]
        psV = [[psE0, psO0], [psE1, psO1]]

        @blk.gpsimd
        def _(gpsimd):
            for idx in range(8, 24):
                sec = 1 if idx < 16 else 2
                dc = idx % 8
                gpsimd.wait_ge(sb.h("w2c"), idx - 1)
                sl = (idx % 2) * 1024
                sb.inc("w2d%d" % (idx % 2), gpsimd.dma_start(
                    wstg[:, sl:sl + 1024],
                    wqkv_d[dc * 128:(dc + 1) * 128,
                           sec * 1024:(sec + 1) * 1024]), 16)
            gpsimd.wait_ge(sb.h("nbx"), 1)
            sb.inc("nbx", gpsimd.tensor_copy(nbg[:, :], nbt[:, 0:1]))

        @blk.sync
        def _(sync):
            sync.wait_ge(sb.h("lo0"), 16 * (NT // 2))
            sync.wait_ge(sb.h("lo1"), 16 * (NT // 2))
            def emit_qout(tsq):
                for cq in range(4):
                    itq = tsq * 4 + cq
                    sync.wait_ge(sb.h("p2c"), itq + 1)
                    sl2 = (itq % 2) * 1024
                    nm = "q2o" if itq % 2 == 0 else "q2o1"
                    sb.inc(nm, sync.dma_start(
                        qs_r[:, cq, tsq * 512:(tsq + 1) * 512],
                        qout[:, sl2:sl2 + 512]), 16)
                    sb.inc(nm, sync.dma_start(
                        qs_r[:, 4 + cq, tsq * 512:(tsq + 1) * 512],
                        qout[:, sl2 + 512:sl2 + 1024]), 16)

            for sec in range(2):
                nts = 4 if sec == 0 else 8
                for ts in range(nts):
                    gt = gts(sec, ts)
                    if sec == 0 and ts >= 1:
                        emit_qout(ts - 1)
                    if gt >= 2:
                        sync.wait_ge(sb.h("p2m"), 4 * (gt - 1))
                    sl = (gt % 2)
                    sb.inc("l2d%d" % sl, sync.dma_start(
                        lnbuf2[:, sl * 4096:sl * 4096 + 4096].rearrange(
                            "p (dc u) -> p dc u", u=512),
                        ln1s_r[:, :, ts * 512:ts * 512 + 512]), 16)
                    if gt >= 2:
                        sync.wait_ge(sb.h("d2m"), 4 * (gt - 1))
                    cs_src = csq_d if sec == 0 else csk_d
                    sn_src = snq_d if sec == 0 else snk_d
                    sb.inc("c2d%d" % sl, sync.dma_start(
                        csb[:, sl * 512:sl * 512 + 512],
                        cs_src[:, ts * 512:(ts + 1) * 512]), 16)
                    sb.inc("c2d%d" % sl, sync.dma_start(
                        snb[:, sl * 512:sl * 512 + 512],
                        sn_src[:, ts * 512:(ts + 1) * 512]), 16)
                if sec == 0:
                    emit_qout(3)
            for tc in range(NT):
                if tc >= 2:
                    sync.wait_ge(sb.h("p2v"), tc - 1)
                sb.inc("v2d%d" % (tc % 2), sync.dma_start(
                    lnvb[:, (tc % 2) * DM:(tc % 2) * DM + DM].rearrange(
                        "p (dc u) -> p dc u", u=128),
                    ln1s_r[:, :, tc * 128:(tc + 1) * 128]), 16)
            sync.wait_ge(sb.h("q2o"), 16 * 16)
            sync.wait_ge(sb.h("q2o1"), 16 * 16)
            sb.inc("q3d", sync.dma_start(
                q_qc.rearrange("p (ch u) -> p ch u", u=512),
                qs_r[:, :, 0:512]), 16)

        @blk.tensor
        def _(tensor):
            for (sec, ts, c) in QK_ITERS:
                g2 = giter(sec, ts, c)
                gt = gts(sec, ts)
                tensor.wait_ge(sb.h("w2c"), 8 * (sec + 1))
                sl = gt % 2
                tensor.wait_ge(sb.h("l2d%d" % sl), 16 * (gt // 2 + 1))
                if g2 >= 2:
                    tensor.wait_ge(sb.h("d2m"), g2 - 1)
                psE = psEs[g2 % 2]
                psO = psOs[g2 % 2]
                for dc in range(NDC):
                    tensor.matmul(
                        psE[:, :],
                        wqk_bf[:, dc * 1024 + c * 128:dc * 1024 + c * 128 + 128],
                        lnbuf2[:, sl * 4096 + dc * 512:sl * 4096 + (dc + 1) * 512],
                        start=(dc == 0), stop=(dc == NDC - 1))
                last = None
                for dc in range(NDC):
                    o = dc * 1024 + 512 + c * 128
                    last = tensor.matmul(
                        psO[:, :], wqk_bf[:, o:o + 128],
                        lnbuf2[:, sl * 4096 + dc * 512:sl * 4096 + (dc + 1) * 512],
                        start=(dc == 0), stop=(dc == NDC - 1))
                sb.inc("p2m", last)
                if g2 >= 2:
                    pidx = g2 - 2
                    tensor.wait_ge(sb.h("a2q"), pidx + 1)
                    if pidx >= 1:
                        tensor.wait_ge(sb.h("d2x"), pidx)
                    tensor.matmul(psN[:, :], ones4[:, :], sqeo[:, 0:512],
                                  start=True, stop=False)
                    sb.inc("p2n", tensor.matmul(
                        psN[:, :], ones4[:, :], sqeo[:, 512:1024],
                        start=False, stop=True))
            for pidx in (46, 47):
                tensor.wait_ge(sb.h("a2q"), pidx + 1)
                tensor.wait_ge(sb.h("d2x"), pidx)
                tensor.matmul(psN[:, :], ones4[:, :], sqeo[:, 0:512],
                              start=True, stop=False)
                sb.inc("p2n", tensor.matmul(
                    psN[:, :], ones4[:, :], sqeo[:, 512:1024],
                    start=False, stop=True))
            tensor.wait_ge(sb.h("w2c"), 24)
            for tc in range(NT):
                tensor.wait_ge(sb.h("v2d%d" % (tc % 2)), 16 * (tc // 2 + 1))
                if tc >= 2:
                    tensor.wait_ge(sb.h("d2v"), tc - 1)
                lhs = lnvb[:, (tc % 2) * DM:(tc % 2) * DM + DM]
                last = None
                for bank in range(2):
                    for dc in range(NDC):
                        o = dc * 1024 + bank * 512
                        last = tensor.matmul(
                            psV[tc % 2][bank][:, :],
                            lhs[:, dc * 128:(dc + 1) * 128],
                            wqk_bf[:, o:o + 512],
                            start=(dc == 0), stop=(dc == NDC - 1))
                sb.inc("p2v", last)
            tensor.wait_ge(sb.h("nbx"), 2)
            sb.inc("nbx", tensor.transpose(psTb[0:1, 0:128], nbg[:, :],
                                           ident_b[:, :]))
            tensor.wait_ge(sb.h("nbx"), 5)
            sb.inc("nbx", tensor.matmul(psN[:, 0:1], ones_b[0:1, :],
                                        nbb[0:1, 0:1], start=True, stop=True))

        def w2cast(vector, idx):
            vector.wait_ge(sb.h("w2d%d" % (idx % 2)), 16 * (idx // 2 + 1))
            if idx >= 16:
                vector.wait_ge(sb.h("p2m"), 48)
            elif idx >= 8:
                vector.wait_ge(sb.h("p2m"), 16)
            dc = idx % 8
            sl = (idx % 2) * 1024
            sb.inc("w2c", vector.tensor_copy(
                wqk_bf[:, dc * 1024:(dc + 1) * 1024], wstg[:, sl:sl + 1024]))

        @blk.vector
        def _(vector):
            vector.wait_ge(sb.h("lo0"), 16 * (NT // 2))
            vector.wait_ge(sb.h("lo1"), 16 * (NT // 2))
            for (sec, ts, c) in QK_ITERS:
                g2 = giter(sec, ts, c)
                if g2 == 16:
                    for idx in range(8, 16):
                        w2cast(vector, idx)
                gt = gts(sec, ts)
                vector.wait_ge(sb.h("p2m"), g2 + 1)
                sl = gt % 2
                vector.wait_ge(sb.h("c2d%d" % sl), 32 * (gt // 2 + 1))
                psE = psEs[g2 % 2]
                psO = psOs[g2 % 2]
                cs = csb[:, sl * 512:sl * 512 + 512]
                sn = snb[:, sl * 512:sl * 512 + 512]
                bE = bqkv_pt[:, sec * 8 + c:sec * 8 + c + 1]
                bO = bqkv_pt[:, sec * 8 + 4 + c:sec * 8 + 4 + c + 1]
                if sec == 0:
                    sl2 = (g2 % 2) * 1024
                    oE = qout[:, sl2:sl2 + 512]
                    oO = qout[:, sl2 + 512:sl2 + 1024]
                else:
                    oE = k_fm[:, c * S + ts * 512:c * S + ts * 512 + 512]
                    oO = k_fm[:, (4 + c) * S + ts * 512:
                              (4 + c) * S + ts * 512 + 512]
                if g2 >= 1:
                    vector.wait_ge(sb.h("p2c"), g2)
                vector.scalar_tensor_tensor(
                    scrm[:, 0:512], psE[:, :], bE, cs, ALU.add, ALU.mult)
                sb.inc("r2e", vector.scalar_tensor_tensor(
                    scrm[:, 512:1024], psO[:, :], bO, sn, ALU.add, ALU.mult))
                if sec == 0 and g2 >= 2:
                    vector.wait_ge(sb.h("q2o" if g2 % 2 == 0 else "q2o1"),
                                   32 * (g2 // 2))
                    vector.wait_ge(sb.h("a2q"), g2 - 1)
                vector.wait_ge(sb.h("r2e"), g2 + 1)
                sb.inc("r2s", vector.tensor_sub(
                    oE, scrm[:, 0:512], scrm[:, 512:1024]))
                vector.wait_ge(sb.h("r2s"), g2 + 1)
                vector.scalar_tensor_tensor(
                    scrm[:, 0:512], psE[:, :], bE, sn, ALU.add, ALU.mult)
                sb.inc("d2m", vector.scalar_tensor_tensor(
                    scrm[:, 512:1024], psO[:, :], bO, cs, ALU.add, ALU.mult))
                vector.wait_ge(sb.h("d2m"), g2 + 1)
                sb.inc("p2c", vector.tensor_add(
                    oO, scrm[:, 0:512], scrm[:, 512:1024]))
                if g2 >= 2:
                    pidx = g2 - 2
                    vector.wait_ge(sb.h("p2n"), pidx + 1)
                    sb.inc("d2x", vector.reduce_max(
                        normc[:, pidx:pidx + 1], psN[:, :], axis=AX.X))
            for pidx in (46, 47):
                vector.wait_ge(sb.h("p2n"), pidx + 1)
                sb.inc("d2x", vector.reduce_max(
                    normc[:, pidx:pidx + 1], psN[:, :], axis=AX.X))
            for idx in range(16, 24):
                w2cast(vector, idx)
            for tc in range(NT):
                vector.wait_ge(sb.h("p2v"), tc + 1)
                base = tc * 16 * VW
                dst0 = v_tm[:, base:base + 8 * VW].rearrange(
                    "p (h u) -> p h u", u=VW)[:, :, 0:64]
                dst1 = v_tm[:, base + 8 * VW:base + 16 * VW].rearrange(
                    "p (h u) -> p h u", u=VW)[:, :, 0:64]
                src0 = psV[tc % 2][0][:, :].rearrange("p (h u) -> p h u", u=64)
                src1 = psV[tc % 2][1][:, :].rearrange("p (h u) -> p h u", u=64)
                bvb0 = bvb_t[:, 0:512].rearrange("p (h u) -> p h u", u=64)
                bvb1 = bvb_t[:, 512:1024].rearrange("p (h u) -> p h u", u=64)
                vector.tensor_add(dst0, src0, bvb0)
                sb.inc("d2v", vector.tensor_add(dst1, src1, bvb1))
            vector.wait_ge(sb.h("d2x"), 48)
            sb.inc("nbx", vector.reduce_max(nbt[:, 0:1], normc[:, 0:48],
                                            axis=AX.X))
            vector.wait_ge(sb.h("nbx"), 3)
            sb.inc("nbx", vector.reduce_max(nbt[0:1, 2:3], psTb[0:1, 0:128],
                                            axis=AX.X))
            vector.wait_ge(sb.h("nbx"), 6)
            sb.inc("negB", vector.tensor_copy(negB_t[:, :], psN[:, 0:1]))

        @blk.scalar
        def _(scalar):
            for (sec, ts, c) in QK_ITERS:
                g2 = giter(sec, ts, c)
                scalar.wait_ge(sb.h("p2c"), g2 + 1)
                if g2 >= 1:
                    scalar.wait_ge(sb.h("p2n"), g2)
                if sec == 0:
                    sl2 = (g2 % 2) * 1024
                    iE = qout[:, sl2:sl2 + 512]
                    iO = qout[:, sl2 + 512:sl2 + 1024]
                else:
                    iE = k_fm[:, c * S + ts * 512:c * S + ts * 512 + 512]
                    iO = k_fm[:, (4 + c) * S + ts * 512:
                              (4 + c) * S + ts * 512 + 512]
                scalar.activation(sqeo[:, 0:512], iE, AF.Square)
                sb.inc("a2q", scalar.activation(sqeo[:, 512:1024], iO,
                                                AF.Square))
            scalar.wait_ge(sb.h("nbx"), 4)
            sb.inc("nbx", scalar.activation(nbb[0:1, 0:1], nbt[0:1, 2:3],
                                            AF.Copy, scale=-0.126))

    st_bvb.close()
    st_w2.close()

    if phases <= 2:
        _finish(nc, sb, [("q2o", 16 * 32), ("d2v", 32), ("negB", 1)])
        return nc

    st3 = ExitStack()
    attn_fm = st3.enter_context(
        nc.sbuf_tensor("attn_fm", [128, 8 * OWN], BF16, side="right"))
    with (
        nc.sbuf_tensor("ring", [128, 5 * 2048], BF16, side="right") as ring,
        nc.sbuf_tensor("scr_un", [128, 1024], F32, side="right") as scr_un,
        nc.sbuf_tensor("den_sb", [1, 2048], F32, side="right") as den_sb,
        nc.sbuf_tensor("wcf", [128, 512], F32, side="right") as wcf,
        nc.sbuf_tensor("wcb", [128, 512], BF16, side="right") as wcb,
        nc.psum_tensor("psc", [128, 2048], F32) as psc,
        nc.psum_tensor("pwvA", [128, 2048], F32) as pwvA,
        nc.Block() as blk,
    ):
        pwv = [pwvA[:, r * 512:(r + 1) * 512] for r in range(4)]

        @blk.sync
        def _(sync):
            sync.wait_ge(sb.h("q2o"), 16 * 16)
            sync.wait_ge(sb.h("q2o1"), 16 * 16)

            def q3dma(qc):
                sb.inc("q3d", sync.dma_start(
                    q_qc.rearrange("p (ch u) -> p ch u", u=512),
                    qs_r[:, :, qc * 512:(qc + 1) * 512]), 16)

            for qc in range(1, n_qc_run):
                sync.wait_ge(sb.h("s3"), NG * KC * qc)
                q3dma(qc)

        @blk.gpsimd
        def _(gpsimd):
            chunks = []
            for cc in range(NDC):
                for j in range(2):
                    chunks.append((
                        wo_d[cc * 128:(cc + 1) * 128, j * 512:(j + 1) * 512],
                        wob_s[:, cc * 1024 + j * 512:cc * 1024 + j * 512 + 512]))
            for hc in range(32):
                for j in range(2):
                    chunks.append((
                        w2_d[hc * 128:(hc + 1) * 128, j * 512:(j + 1) * 512],
                        w2b_s[:, hc * 1024 + j * 512:hc * 1024 + j * 512 + 512]))
            for dc in range(NDC):
                for h in range(8):
                    chunks.append((
                        w1_d[dc * 128:(dc + 1) * 128, h * 512:(h + 1) * 512],
                        w1b_s[:, h * 4096 + dc * 512:h * 4096 + dc * 512 + 512]))
            for i, (src, dst) in enumerate(chunks):
                if i >= 1:
                    gpsimd.wait_ge(sb.h("wcc"), i)
                sb.inc("wcd", gpsimd.dma_start(wcf[:, :], src), 16)
                gpsimd.wait_ge(sb.h("wcd"), 16 * (i + 1))
                if i >= 1:
                    gpsimd.wait_ge(sb.h("wco"), 16 * i)
                sb.inc("wcc", gpsimd.tensor_copy(wcb[:, :], wcf[:, :]))
                gpsimd.wait_ge(sb.h("wcc"), i + 1)
                sb.inc("wco", gpsimd.dma_start(dst, wcb[:, :]), 16)

        def sc_mm(tensor, r, g, kc, first):
            rr = slice(32 * r, 32 * r + 32)
            ch = g if first else 4 + g
            return tensor.matmul(
                psc[:, r * 512:(r + 1) * 512],
                k_fm[rr, ch * S + kc * 128:ch * S + kc * 128 + 128],
                q_qc[rr, ch * 512:(ch + 1) * 512],
                start=first, stop=not first, tile_position=(32 * r, 0))

        def bc_mm(tensor, r):
            if r % 2 == 0:
                return tensor.matmul(
                    pwv[r][0:64, 0:512], ones_f[0:1, 0:64],
                    den_sb[0:1, r * 512:r * 512 + 512],
                    start=True, stop=True)
            return tensor.matmul(
                pwv[r][64:128, 0:512], ones_f[0:1, 0:64],
                den_sb[0:1, r * 512:r * 512 + 512],
                start=True, stop=True, tile_position=(0, 64))

        @blk.tensor
        def _(tensor):
            for qc in range(n_qc_run):
                tensor.wait_ge(sb.h("q3d"), 16 * (qc + 1))
                for g in range(NG):
                    G = qc * NG + g
                    if G >= att_g:
                        break
                    for kc in range(KC):
                        n = G * KC + kc
                        if n >= 1:
                            tensor.wait_ge(sb.h("e3a"), n)
                        sc_mm(tensor, 0, g, kc, True)
                        sc_mm(tensor, 0, g, kc, False)
                        sc_mm(tensor, 1, g, kc, True)
                        last = sc_mm(tensor, 1, g, kc, False)
                        sb.inc("s3a", last)
                        if n >= 1:
                            tensor.wait_ge(sb.h("e3b"), n)
                        sc_mm(tensor, 2, g, kc, True)
                        sc_mm(tensor, 2, g, kc, False)
                        sc_mm(tensor, 3, g, kc, True)
                        last = sc_mm(tensor, 3, g, kc, False)
                        sb.inc("s3", last)
                        if kc == 3 and G >= 1:
                            tensor.wait_ge(sb.h("ev3"), 2 * G - 1)
                            tensor.wait_ge(sb.h("dnd"), 2 * G)
                            sb.inc("b3", bc_mm(tensor, 0))
                            sb.inc("b3", bc_mm(tensor, 1))
                            tensor.wait_ge(sb.h("ev3"), 2 * G)
                            sb.inc("b3", bc_mm(tensor, 2))
                            sb.inc("b3", bc_mm(tensor, 3))
                        if kc >= 4:
                            kcw = kc - 4
                            nw = G * KC + kcw
                            tensor.wait_ge(sb.h("e3b"), nw + 1)
                            if kcw == 0 and G >= 1:
                                tensor.wait_ge(sb.h("n3"), 2 * G)
                            slot = (nw % 5) * 2048
                            wlast = None
                            for r in range(4):
                                h = 4 * g + r
                                wlast = tensor.matmul(
                                    pwv[r][0:VW, 0:512],
                                    v_tm[:, kcw * 16 * VW + h * VW:
                                         kcw * 16 * VW + h * VW + VW],
                                    ring[:, slot + r * 512:slot + (r + 1) * 512],
                                    start=(kcw == 0), stop=False)
                            sb.inc("w3", wlast)
                    for kcw in (KC - 4, KC - 3, KC - 2, KC - 1):
                        nw = G * KC + kcw
                        tensor.wait_ge(sb.h("e3b"), nw + 1)
                        slot = (nw % 5) * 2048
                        wlast = None
                        for r in range(4):
                            h = 4 * g + r
                            wlast = tensor.matmul(
                                pwv[r][0:VW, 0:512],
                                v_tm[:, kcw * 16 * VW + h * VW:
                                     kcw * 16 * VW + h * VW + VW],
                                ring[:, slot + r * 512:slot + (r + 1) * 512],
                                start=False, stop=(kcw == KC - 1))
                        sb.inc("w3", wlast)
            G = att_g - 1
            tensor.wait_ge(sb.h("ev3"), 2 * G + 1)
            tensor.wait_ge(sb.h("dnd"), 2 * G + 2)
            sb.inc("b3", bc_mm(tensor, 0))
            sb.inc("b3", bc_mm(tensor, 1))
            tensor.wait_ge(sb.h("ev3"), 2 * G + 2)
            sb.inc("b3", bc_mm(tensor, 2))
            sb.inc("b3", bc_mm(tensor, 3))

        def den_copies(scalar, Gp):
            scalar.wait_ge(sb.h("w3"), KC * (Gp + 1))
            scalar.wait_ge(sb.h("b3"), 4 * Gp)
            sb.inc("dnd", scalar.activation(den_sb[0:1, :],
                                            pwvA[64:65, 0:2048], AF.Ln))
            scalar.wait_ge(sb.h("dnd"), 2 * Gp + 1)
            sb.inc("dnd", scalar.activation(den_sb[0:1, :], den_sb[0:1, :],
                                            AF.Exp, scale=-1.0))

        @blk.scalar
        def _(scalar):
            scalar.wait_ge(sb.h("negB"), 1)
            for n in range(att_g * KC):
                slot = (n % 5) * 2048
                if n % KC == 2 and n >= KC:
                    den_copies(scalar, n // KC - 1)
                scalar.wait_ge(sb.h("s3a"), n + 1)
                if n >= 5:
                    scalar.wait_ge(sb.h("w3"), n - 4)
                sb.inc("e3a", scalar.activation(
                    ring[:, slot:slot + 1024], psc[:, 0:1024], AF.Exp,
                    bias=negB_t[:, :], scale=0.125))
                scalar.wait_ge(sb.h("s3"), n + 1)
                sb.inc("e3b", scalar.activation(
                    ring[:, slot + 1024:slot + 2048], psc[:, 1024:2048], AF.Exp,
                    bias=negB_t[:, :], scale=0.125))
            den_copies(scalar, att_g - 1)

        @blk.vector
        def _(vector):
            for G in range(att_g):
                qc, g = G // NG, G % NG
                vector.wait_ge(sb.h("w3"), KC * (G + 1))
                for half in range(2):
                    r0 = 2 * half
                    if half == 1:
                        vector.wait_ge(sb.h("n3"), 2 * G + 1)
                    vector.tensor_copy(scr_un[0:64, 0:512],
                                       pwv[r0][0:64, 0:512])
                    sb.inc("ev3", vector.tensor_copy(
                        scr_un[64:128, 512:1024], pwv[r0 + 1][0:64, 0:512]))
                    vector.wait_ge(sb.h("b3"), 4 * G + 2 * (half + 1))
                    last = None
                    for j in range(2):
                        r = r0 + j
                        h = 4 * g + r
                        dst = attn_fm[(h % 2) * 64:(h % 2) * 64 + 64,
                                      (h // 2) * OWN + qc * 512:
                                      (h // 2) * OWN + qc * 512 + 512]
                        rows = slice(0, 64) if j == 0 else slice(64, 128)
                        last = vector.tensor_tensor(
                            dst, scr_un[rows, j * 512:(j + 1) * 512],
                            pwv[r][rows, 0:512], ALU.mult)
                    sb.inc("n3", last)

    st_kv.close()

    st_bias = ExitStack()
    bo_t = st_bias.enter_context(nc.sbuf_tensor("bo_s", [128, DM], F32, side="right"))
    gam_t = st_bias.enter_context(nc.sbuf_tensor("gam_s", [128, DM], F32, side="right"))
    bet_t = st_bias.enter_context(nc.sbuf_tensor("bet_s", [128, DM], F32, side="right"))
    st_w5 = ExitStack()
    w1bf = st_w5.enter_context(
        nc.sbuf_tensor("w1bf", [128, 2 * 4096], BF16, side="left"))
    w2bf = st_w5.enter_context(
        nc.sbuf_tensor("w2bf", [128, 32 * 1024], BF16, side="left"))

    if phases <= 3:
        _finish(nc, sb, [("n3", 2 * att_g)])
        return nc

    with (
        nc.sbuf_tensor("wo_bf", [128, 8192], BF16, side="right") as wo_bf,
        nc.sbuf_tensor("xt4", [128, 2 * DM], F32, side="right") as xt4,
        nc.sbuf_tensor("scr_a", [128, DM], F32, side="right") as scr_a,
        nc.sbuf_tensor("scr_h", [128, DM], F32, side="right") as scr_h,
        nc.sbuf_tensor("scr_t", [128, DM], F32, side="right") as scr_t,
        nc.sbuf_tensor("scr_b", [128, DM], F32, side="right") as scr_b,
        nc.sbuf_tensor("x1b", [128, 2 * DM], F32, side="right") as x1b,
        nc.sbuf_tensor("lnbf3", [128, 2 * DM], BF16, side="right") as lnbf3,
        nc.sbuf_tensor("stg3", [128, 2 * DM], BF16, side="right") as stg3,
        nc.sbuf_tensor("jnk4s", [128, 1024], F32, side="right") as junk,
        nc.psum_tensor("pwoa", [128, 1024], F32) as pwoa,
        nc.psum_tensor("pwob", [128, 1024], F32) as pwob,
        nc.psum_tensor("pwoc", [128, 1024], F32) as pwoc,
        nc.psum_tensor("pst4a", [128, 1024], BF16) as pst4a,
        nc.psum_tensor("pst4b", [128, 1024], BF16) as pst4b,
        nc.Block() as blk,
    ):
        pwo = [pwoa, pwob, pwoc]
        pst4 = [pst4a, pst4b]
        x4 = lambda t: xt4[:, (t % 2) * DM:(t % 2) * DM + DM]
        x1t = lambda t: x1b[:, (t % 2) * DM:(t % 2) * DM + DM]
        l3 = lambda t: lnbf3[:, (t % 2) * DM:(t % 2) * DM + DM]
        g3 = lambda t: stg3[:, (t % 2) * DM:(t % 2) * DM + DM]

        @blk.sync
        def _(sync):
            sync.wait_ge(sb.h("n3"), 2 * NQC * NG)
            sync.wait_ge(sb.h("wco"), 16 * 144)
            for src, dst in [(bo_d, bo_t), (gam_d, gam_t), (bet_d, bet_t)]:
                sb.inc("P0d", sync.dma_start(dst[:, :], src), 16)
            for t in range(NTO):
                if t >= 2:
                    sync.wait_ge(sb.h("p4x"), t - 1)
                sb.inc("x4d%d" % (t % 2), sync.dma_start(
                    x4(t), x_d[t * 128:(t + 1) * 128, :]), 16)
                if t >= 1:
                    sync.wait_ge(sb.h("p4x"), t)
                    sb.inc("sx1%d" % ((t - 1) % 2), sync.dma_start(
                        x1_s[:, (t - 1) * DM:t * DM], x1t(t - 1)), 16)
                if t >= 2:
                    sync.wait_ge(sb.h("a4V"), t - 1)
                    sb.inc("s3o%d" % ((t - 2) % 2), sync.dma_start(
                        ln3s_r[:, :, (t - 2) * 128:(t - 1) * 128],
                        g3(t - 2).rearrange("p (dc u) -> p dc u", u=128)), 16)
            sync.wait_ge(sb.h("p4x"), NTO)
            sb.inc("sx1%d" % ((NTO - 1) % 2), sync.dma_start(
                x1_s[:, (NTO - 1) * DM:NTO * DM], x1t(NTO - 1)), 16)
            for tp in (NTO - 2, NTO - 1):
                sync.wait_ge(sb.h("a4V"), tp + 1)
                sb.inc("s3o%d" % (tp % 2), sync.dma_start(
                    ln3s_r[:, :, tp * 128:(tp + 1) * 128],
                    g3(tp).rearrange("p (dc u) -> p dc u", u=128)), 16)

        @blk.gpsimd
        def _(gpsimd):
            gpsimd.wait_ge(sb.h("P0d"), sb.n("P0d"))
            gpsimd.wait_ge(sb.h("n3"), 2 * NQC * NG)
            gpsimd.wait_ge(sb.h("wco"), 16 * 144)
            for half in range(2):
                sb.inc("w4d", gpsimd.dma_start(
                    wo_bf[:, half * 4096:(half + 1) * 4096],
                    wob_s[:, half * 4096:(half + 1) * 4096]), 16)
            for t in range(NTO):
                gpsimd.wait_ge(sb.h("d4u"), t + 1)
                sb.inc("p4b", gpsimd.tensor_add(scr_b[:, :], scr_a[:, :],
                                                bet_t[:, :]))
                gpsimd.wait_ge(sb.h("p4b"), t + 1)
                if t >= 2:
                    gpsimd.wait_ge(sb.h("sx1%d" % (t % 2)), 16 * (t // 2))
                    gpsimd.wait_ge(sb.h("d4n"), t - 1)
                    gpsimd.wait_ge(sb.h("a4q3"), t - 1)
                sb.inc("p4x", gpsimd.tensor_add(x1t(t), scr_b[:, :], x4(t)))
            gpsimd.wait_ge(sb.h("wco"), 16 * 80)
            for j in range(8):
                sb.inc("w52d", gpsimd.dma_start(
                    w2bf[:, j * 4096:(j + 1) * 4096],
                    w2b_s[:, j * 4096:(j + 1) * 4096]), 16)
            gpsimd.wait_ge(sb.h("wco"), 16 * 144)
            for gg in range(2):
                sb.inc("w51d", gpsimd.dma_start(
                    w1bf[:, gg * 4096:(gg + 1) * 4096],
                    w1b_s[:, gg * 4096:(gg + 1) * 4096]), 16)

        @blk.tensor
        def _(tensor):
            tensor.wait_ge(sb.h("w4d"), 32)
            tensor.wait_ge(sb.h("n3"), 2 * NQC * NG)
            for t in range(NTO):
                if t >= 3:
                    tensor.wait_ge(sb.h("d4h"), 2 * (t - 2))
                last = None
                for bank in range(2):
                    for cc in range(NDC):
                        o = cc * 1024 + bank * 512
                        last = tensor.matmul(
                            pwo[t % 3][:, bank * 512:bank * 512 + 512],
                            attn_fm[:, cc * OWN + t * 128:cc * OWN + t * 128 + 128],
                            wo_bf[:, o:o + 512],
                            start=(cc == 0), stop=(cc == NDC - 1))
                sb.inc("p4o", last)
                if t >= 1:
                    tp = t - 1
                    tensor.wait_ge(sb.h("d4n"), tp + 1)
                    if tp >= 2:
                        tensor.wait_ge(sb.h("a4V"), tp - 1)
                    tlast = None
                    for dc in range(NDC):
                        tlast = tensor.transpose(
                            pst4[tp % 2][:, dc * 128:(dc + 1) * 128],
                            l3(tp)[:, dc * 128:(dc + 1) * 128], ident_b[:, :])
                    sb.inc("p4T", tlast)
            tp = NTO - 1
            tensor.wait_ge(sb.h("d4n"), tp + 1)
            tensor.wait_ge(sb.h("a4V"), tp - 1)
            tlast = None
            for dc in range(NDC):
                tlast = tensor.transpose(
                    pst4[tp % 2][:, dc * 128:(dc + 1) * 128],
                    l3(tp)[:, dc * 128:(dc + 1) * 128], ident_b[:, :])
            sb.inc("p4T", tlast)

        @blk.vector
        def _(vector):
            vector.wait_ge(sb.h("P0d"), sb.n("P0d"))
            vector.wait_ge(sb.h("n3"), 2 * NQC * NG)
            vector.wait_ge(sb.h("wco"), 16 * 144)
            for t in range(NTO):
                vector.wait_ge(sb.h("x4d%d" % (t % 2)), 16 * (t // 2 + 1))
                if t >= 1:
                    vector.wait_ge(sb.h("p4b"), t)
                sb.inc("d4a", vector.tensor_scalar(
                    scr_a[:, :], x4(t), mean1[:, t:t + 1], rstd1[:, t:t + 1],
                    ALU.subtract, ALU.mult))
                vector.wait_ge(sb.h("p4o"), t + 1)
                vector.wait_ge(sb.h("d4a"), t + 1)
                if t >= 1:
                    vector.wait_ge(sb.h("a4q2"), t)
                sb.inc("d4h", vector.tensor_add(scr_h[:, :], pwo[t % 3][:, :],
                                                scr_a[:, :]))
                vector.wait_ge(sb.h("d4h"), 2 * t + 1)
                sb.inc("d4h", vector.tensor_add(scr_h[:, :], scr_h[:, :],
                                                bo_t[:, :]))
                vector.wait_ge(sb.h("d4h"), 2 * t + 2)
                sb.inc("d4r2", vector.reduce_sum(s1b[:, t:t + 1], scr_h[:, :],
                                                 axis=AX.X))
                vector.wait_ge(sb.h("p4s2"), t + 1)
                vector.wait_ge(sb.h("a4q2"), t + 1)
                sb.inc("d4v2", vector.tensor_scalar(
                    vr2[:, t:t + 1], s2b[:, t:t + 1], t1b[:, t:t + 1], INV_DM,
                    ALU.subtract, ALU.mult))
                vector.wait_ge(sb.h("a4e2"), 2 * (t + 1))
                sb.inc("d4t", vector.tensor_scalar(
                    scr_t[:, :], scr_h[:, :], mn2[:, t:t + 1], rs2[:, t:t + 1],
                    ALU.subtract, ALU.mult))
                vector.wait_ge(sb.h("d4t"), t + 1)
                sb.inc("d4u", vector.tensor_mul(scr_a[:, :], scr_t[:, :],
                                                gam_t[:, :]))
                vector.wait_ge(sb.h("p4x"), t + 1)
                sb.inc("d4r3", vector.reduce_sum(s1d[:, t:t + 1], x1t(t),
                                                 axis=AX.X))
                vector.wait_ge(sb.h("p4s3"), t + 1)
                vector.wait_ge(sb.h("a4q3"), t + 1)
                sb.inc("d4v3", vector.tensor_scalar(
                    vr3[:, t:t + 1], s2d[:, t:t + 1], t1d[:, t:t + 1], INV_DM,
                    ALU.subtract, ALU.mult))
                vector.wait_ge(sb.h("a4e3"), 2 * (t + 1))
                if t >= 2:
                    vector.wait_ge(sb.h("p4T"), t - 1)
                sb.inc("d4n", vector.tensor_scalar(
                    l3(t), x1t(t), mn3[:, t:t + 1], rs3[:, t:t + 1],
                    ALU.subtract, ALU.mult))

        @blk.scalar
        def _(scalar):
            for t in range(NTO):
                scalar.wait_ge(sb.h("d4h"), 2 * t + 2)
                sb.inc("a4q2", scalar.activation(
                    junk[:, :], scr_h[:, :], AF.Square,
                    accum_out=s2b[:, t:t + 1]))
                scalar.wait_ge(sb.h("d4r2"), t + 1)
                scalar.activation(mn2[:, t:t + 1], s1b[:, t:t + 1], AF.Copy,
                                  scale=INV_DM)
                sb.inc("p4s2", scalar.activation(
                    t1b[:, t:t + 1], s1b[:, t:t + 1], AF.Square, scale=0.03125))
                scalar.wait_ge(sb.h("d4v2"), t + 1)
                sb.inc("a4e2", scalar.activation(
                    lv2[:, t:t + 1], vr2[:, t:t + 1], AF.Ln, bias=eps_t[:, :]))
                scalar.wait_ge(sb.h("a4e2"), 2 * t + 1)
                sb.inc("a4e2", scalar.activation(
                    rs2[:, t:t + 1], lv2[:, t:t + 1], AF.Exp, scale=-0.5))
                scalar.wait_ge(sb.h("p4x"), t + 1)
                sb.inc("a4q3", scalar.activation(
                    junk[:, :], x1t(t), AF.Square, accum_out=s2d[:, t:t + 1]))
                scalar.wait_ge(sb.h("d4r3"), t + 1)
                scalar.activation(mn3[:, t:t + 1], s1d[:, t:t + 1], AF.Copy,
                                  scale=INV_DM)
                sb.inc("p4s3", scalar.activation(
                    t1d[:, t:t + 1], s1d[:, t:t + 1], AF.Square, scale=0.03125))
                scalar.wait_ge(sb.h("d4v3"), t + 1)
                sb.inc("a4e3", scalar.activation(
                    lv3[:, t:t + 1], vr3[:, t:t + 1], AF.Ln, bias=eps_t[:, :]))
                scalar.wait_ge(sb.h("a4e3"), 2 * t + 1)
                sb.inc("a4e3", scalar.activation(
                    rs3[:, t:t + 1], lv3[:, t:t + 1], AF.Exp, scale=-0.5))
                if t >= 1:
                    tp = t - 1
                    scalar.wait_ge(sb.h("p4T"), tp + 1)
                    if tp >= 2:
                        scalar.wait_ge(sb.h("s3o%d" % (tp % 2)), 16 * (tp // 2))
                    sb.inc("a4V", scalar.copy(g3(tp), pst4[tp % 2][:, :]))
            tp = NTO - 1
            scalar.wait_ge(sb.h("p4T"), tp + 1)
            scalar.wait_ge(sb.h("s3o%d" % (tp % 2)), 16 * (tp // 2))
            sb.inc("a4V", scalar.copy(g3(tp), pst4[tp % 2][:, :]))

    st_bias.close()
    st3.close()

    if phases <= 4:
        _finish(nc, sb, [("s3o0", 16 * 8), ("s3o1", 16 * 8),
                         ("sx10", 16 * 8), ("sx11", 16 * 8)])
        return nc

    with (
        nc.sbuf_tensor("ln3b5", [128, 2 * 4096], BF16, side="right") as ln3b5,
        nc.sbuf_tensor("h1", [128, 2 * 16384], BF16, side="right") as h1,
        nc.sbuf_tensor("x1b5", [128, 2 * DM], F32, side="right") as x1b5,
        nc.sbuf_tensor("scr5", [128, 2 * DM], F32, side="right") as scr5,
        nc.sbuf_tensor("b2_s", [128, DM], F32, side="right") as b2_t,
        nc.psum_tensor("pfA", [128, 1024], F32) as pfA,
        nc.psum_tensor("pfB", [128, 1024], F32) as pfB,
        nc.psum_tensor("pfC", [128, 1024], F32) as pfC,
        nc.psum_tensor("pfD", [128, 1024], F32) as pfD,
        nc.Block() as blk,
    ):
        pf2 = [pfC, pfD]

        @blk.gpsimd
        def _(gpsimd):
            for gg in range(2, 32):
                gpsimd.wait_ge(sb.h("p5f1"), 4 * (gg - 1))
                gpsimd.wait_ge(sb.h("w51d"), 16 * gg)
                sb.inc("w51d", gpsimd.dma_start(
                    w1bf[:, (gg % 2) * 4096:(gg % 2) * 4096 + 4096],
                    w1b_s[:, (gg % 8) * 4096:(gg % 8) * 4096 + 4096]), 16)

        @blk.sync
        def _(sync):
            for nm in ("s3o0", "s3o1", "sx10", "sx11"):
                sync.wait_ge(sb.h(nm), 16 * (NTO // 2))
            sb.inc("P0d", sync.dma_start(b2_t[:, :], b2_d), 16)

            def out_dma(sync, tt):
                sync.wait_ge(sb.h("d5e"), 2 * tt + 2)
                sb.inc("o5%d" % (tt % 2), sync.dma_start(
                    out_d[tt * 128:(tt + 1) * 128, :],
                    x1b5[:, (tt % 2) * DM:(tt % 2) * DM + DM]), 16)

            def ln3_dma(sync, bb):
                if bb >= 2:
                    sync.wait_ge(sb.h("p5f1"), 32 * (bb - 1))
                sb.inc("l5d" if bb % 2 == 0 else "l5d1", sync.dma_start(
                    ln3b5[:, (bb % 2) * 4096:(bb % 2) * 4096 + 4096].rearrange(
                        "p (dc u) -> p dc u", u=512),
                    ln3s_r[:, :, bb * 512:(bb + 1) * 512]), 16)

            ln3_dma(sync, 0)
            for b in range(4):
                if b + 1 < 4:
                    ln3_dma(sync, b + 1)
                for tt in range(b * 4, b * 4 + 4):
                    if tt >= 2:
                        out_dma(sync, tt - 2)
                        sync.wait_ge(sb.h("d5e"), 2 * tt - 2)
                        sync.wait_ge(sb.h("o5%d" % (tt % 2)), 16 * (tt // 2))
                    sb.inc("x5d%d" % (tt % 2), sync.dma_start(
                        x1b5[:, (tt % 2) * DM:(tt % 2) * DM + DM],
                        x1_s[:, tt * DM:(tt + 1) * DM]), 16)
            out_dma(sync, 14)
            out_dma(sync, 15)

        @blk.vector
        def _(vector):
            for nm in ("s3o0", "s3o1", "sx10", "sx11"):
                vector.wait_ge(sb.h(nm), 16 * (NTO // 2))
            vector.wait_ge(sb.h("P0d"), sb.n("P0d"))

            def evacs(vector, bb):
                for p in range(2):
                    for tt in range(bb * 4 + p * 2, bb * 4 + p * 2 + 2):
                        tcl = tt - (bb * 4 + p * 2)
                        vector.wait_ge(sb.h("p5f2"), 32 * (bb * 2 + p + 1))
                        vector.wait_ge(sb.h("x5d%d" % (tt % 2)),
                                       16 * (tt // 2 + 1))
                        if tt >= 2:
                            vector.wait_ge(sb.h("d5e"), 2 * tt - 2)
                        sb.inc("d5e", vector.tensor_add(
                            scr5[:, (tt % 2) * DM:(tt % 2) * DM + DM],
                            pf2[tcl][:, :],
                            x1b5[:, (tt % 2) * DM:(tt % 2) * DM + DM]))
                        vector.wait_ge(sb.h("d5e"), 2 * tt + 1)
                        sb.inc("d5e", vector.tensor_add(
                            x1b5[:, (tt % 2) * DM:(tt % 2) * DM + DM],
                            scr5[:, (tt % 2) * DM:(tt % 2) * DM + DM],
                            b2_t[:, :]))

            for b in range(4):
                evacs(vector, b)

        @blk.tensor
        def _(tensor):
            pf1banks = [pfA[:, 0:512], pfA[:, 512:1024],
                        pfB[:, 0:512], pfB[:, 512:1024]]
            tensor.wait_ge(sb.h("w52d"), 16 * 8)
            for b in range(4):
                tensor.wait_ge(sb.h("l5d" if b % 2 == 0 else "l5d1"),
                               16 * (b // 2 + 1))
                for hc in range(32):
                    gg = b * 8 + hc // 4
                    tensor.wait_ge(sb.h("w51d"),
                                   32 if gg < 2 else 16 * (gg + 1))
                    if b * 32 + hc >= 4:
                        tensor.wait_ge(sb.h("a5g"), b * 32 + hc - 3)
                    bank = pf1banks[hc % 4]
                    last = None
                    for dc in range(NDC):
                        o = (gg % 2) * 4096 + dc * 512 + (hc % 4) * 128
                        last = tensor.matmul(
                            bank, w1bf[:, o:o + 128],
                            ln3b5[:, (b % 2) * 4096 + dc * 512:
                                  (b % 2) * 4096 + (dc + 1) * 512],
                            start=(dc == 0), stop=(dc == NDC - 1))
                    sb.inc("p5f1", last)
                for p in range(2):
                    for hc in range(32):
                        tensor.wait_ge(sb.h("a5g"), b * 32 + hc + 1)
                        if hc == 0 and b * 2 + p >= 1:
                            prev_tt = (b * 2 + p) * 2 - 1
                            tensor.wait_ge(sb.h("d5e"), 2 * prev_tt + 1)
                        last = None
                        for tcl in range(2):
                            tok = (p * 2 + tcl) * 128
                            for bank in range(2):
                                last = tensor.matmul(
                                    pf2[tcl][:, bank * 512:bank * 512 + 512],
                                    h1[:, (b % 2) * 16384 + hc * 512 + tok:
                                       (b % 2) * 16384 + hc * 512 + tok + 128],
                                    w2bf[:, hc * 1024 + bank * 512:
                                         hc * 1024 + bank * 512 + 512],
                                    start=(hc == 0), stop=(hc == 31))
                        sb.inc("p5f2", last)

        @blk.scalar
        def _(scalar):
            for b in range(4):
                for hc in range(32):
                    scalar.wait_ge(sb.h("p5f1"), b * 32 + hc + 1)
                    if hc == 0 and b >= 2:
                        scalar.wait_ge(sb.h("p5f2"), 64 * (b - 1))
                    sb.inc("a5g", scalar.activation(
                        h1[:, (b % 2) * 16384 + hc * 512:
                           (b % 2) * 16384 + (hc + 1) * 512],
                        (pfA if hc % 4 < 2 else pfB)[
                            :, (hc % 2) * 512:(hc % 2) * 512 + 512],
                        AF.Gelu, bias=b1_pt[:, hc:hc + 1]))

    with nc.Block() as blk:
        @blk.sync
        def _(sync):
            sync.wait_ge(sb.h("o50"), 16 * (NTO // 2))
            sync.wait_ge(sb.h("o51"), 16 * (NTO // 2))

    return nc


_CACHE = {}


def _host_prep(Wqkv, bqkv):
    perm = []
    for base in (0, DM):
        for eo in (0, 1):
            for c in range(4):
                for hh in range(4):
                    h = 4 * c + hh
                    for i in range(32):
                        perm.append(base + h * 64 + 2 * i + eo)
    perm += list(range(2 * DM, 3 * DM))
    perm = np.asarray(perm)
    Wp = np.ascontiguousarray(Wqkv[:, perm], dtype=np.float32)
    bp = np.ascontiguousarray(bqkv[perm], dtype=np.float32)
    return Wp, bp


def _tables(positions):
    i = np.arange(32, dtype=np.float64)
    inv = 10000.0 ** (-2.0 * i / 64.0)
    ang = positions[None, :].astype(np.float64) * inv[:, None]
    cs = np.tile(np.cos(ang), (4, 1)).astype(np.float32)
    sn = np.tile(np.sin(ang), (4, 1)).astype(np.float32)
    return np.ascontiguousarray(cs), np.ascontiguousarray(sn)


def kernel(L0_attn_x, L0_attn_Wqkv, L0_attn_bqkv, L0_attn_Wo, L0_attn_bo,
           L0_attn_gamma, L0_attn_beta, L0_ffn_w1, L0_ffn_b1, L0_ffn_w2,
           L0_ffn_b2):
    x = np.ascontiguousarray(np.asarray(L0_attn_x), dtype=np.float32)
    Wqkv = np.asarray(L0_attn_Wqkv, dtype=np.float32)
    bqkv = np.asarray(L0_attn_bqkv, dtype=np.float32)
    Wo = np.ascontiguousarray(np.asarray(L0_attn_Wo), dtype=np.float32)
    bo = np.asarray(L0_attn_bo, dtype=np.float32)
    gam = np.asarray(L0_attn_gamma, dtype=np.float32)
    bet = np.asarray(L0_attn_beta, dtype=np.float32)
    W1 = np.ascontiguousarray(np.asarray(L0_ffn_w1), dtype=np.float32)
    b1 = np.asarray(L0_ffn_b1, dtype=np.float32)
    W2 = np.ascontiguousarray(np.asarray(L0_ffn_w2), dtype=np.float32)
    b2 = np.asarray(L0_ffn_b2, dtype=np.float32)

    Wp, bp = _host_prep(Wqkv, bqkv)
    bqkv_pt = np.ascontiguousarray(bp[0:2 * DM].reshape(16, 128).T)
    bvb_t = np.ascontiguousarray(np.tile(bp[2 * DM:3 * DM], (128, 1)))
    bo_t = np.ascontiguousarray(np.tile(bo, (128, 1)))
    b2_t = np.ascontiguousarray(np.tile(b2, (128, 1)))
    gam_t = np.ascontiguousarray(np.tile(gam, (128, 1)))
    bet_t = np.ascontiguousarray(np.tile(bet, (128, 1)))
    b1_pt = np.ascontiguousarray(b1.reshape(32, 128).T)
    ident = np.eye(128, dtype=np.float32)

    in_maps = []
    for core in range(8):
        b, half = core // 2, core % 2
        if half == 0:
            xr = x[b]
            pos = np.arange(S)
        else:
            xr = np.concatenate([x[b, OWN:], x[b, :OWN]], axis=0)
            pos = np.concatenate([np.arange(OWN, S), np.arange(0, OWN)])
        csk, snk = _tables(pos)
        in_maps.append({
            "x": np.ascontiguousarray(xr),
            "wqkv": Wp, "wo": Wo, "w1": W1, "w2": W2,
            "bqkv_pt": bqkv_pt, "bvb_t": bvb_t, "bo_t": bo_t,
            "b1_pt": b1_pt, "b2_t": b2_t, "gam_t": gam_t, "bet_t": bet_t,
            "csq": np.ascontiguousarray(csk[:, 0:OWN]),
            "snq": np.ascontiguousarray(snk[:, 0:OWN]),
            "csk": csk, "snk": snk, "ident": ident,
        })

    if "nc" not in _CACHE:
        _CACHE["nc"] = build_nc()
    nc = _CACHE["nc"]
    res = run_bass_kernel_spmd(nc, in_maps, core_ids=list(range(8)))

    out = np.empty((4, S, DM), dtype=np.float32)
    for core in range(8):
        b, half = core // 2, core % 2
        out[b, half * OWN:(half + 1) * OWN] = res.results[core]["out"]
    return out
